# version 34
# baseline (speedup 1.0000x reference)
import numpy as np
import concourse.bass as bass
import concourse.mybir as mybir
from concourse.bass_utils import run_bass_kernel_spmd

F32 = mybir.dt.float32
BF16 = mybir.dt.bfloat16
AF = mybir.ActivationFunctionType
ALU = mybir.AluOpType
f32 = np.float32

NCORES = 8
D = 1024
DC = 8
T = 8
I = 256
NR = T * I
H = 16
HD = 64
PAIRS = 8
JT = 16
EXP_SHIFT = -72.0

KB = 1024

OFF_XN = 0
OFF_WREC = 32 * KB
OFF_WF, OFF_WI, OFF_WC = OFF_WREC, OFF_WREC + 16 * KB, OFF_WREC + 32 * KB
OFF_UF, OFF_UI, OFF_UC = OFF_WREC + 48 * KB, OFF_WREC + 64 * KB, OFF_WREC + 80 * KB
OFF_WO, OFF_UO, OFF_WP = OFF_WREC, OFF_WREC + 16 * KB, OFF_WREC + 32 * KB
OFF_CPART = OFF_WREC + 48 * KB
OFF_OSTREAM = OFF_WREC + 56 * KB
OFF_EG = OFF_WREC + 72 * KB
OFF_C32 = 128 * KB
OFF_C16 = 136 * KB
OFF_FG = 140 * KB
OFF_IG = 148 * KB
OFF_GR = 156 * KB
OFF_ATTN = 160 * KB
OFF_BIAS = 200 * KB

A_Q = OFF_ATTN
A_K = OFF_ATTN + 8 * KB
A_V = OFF_ATTN + 16 * KB
A_E = OFF_ATTN + 25 * KB
A_ZR = OFF_ATTN + 29 * KB
A_ZB = OFF_ATTN + 31 * KB
A_ON = OFF_ATTN + 32 * KB
A_ONES = OFF_ATTN + 33 * KB
L_X = OFF_ATTN
L_XB = OFF_ATTN + 16 * KB
L_XQ = OFF_ATTN + 20 * KB
L_W = OFF_ATTN + 24 * KB
L_ROW = OFF_ATTN + 30 * KB
Q_WS = OFF_ATTN
Q_WB = OFF_ATTN + 8 * KB
Q_WV = OFF_ATTN + 12 * KB
Q_ST = OFF_ATTN + 28 * KB
RW_ST = OFF_FG


class Sem:
    def __init__(self, nc, name, dma, owner):
        self.name, self.dma, self.owner = name, dma, owner
        self.h = nc._sem_cm.enter_context(nc.semaphore(name))
        self.n = 0


class Prog:
    ENGINES = ("sync", "vector", "scalar", "tensor", "gpsimd")

    def __init__(self, nc):
        self.nc = nc
        self.ops = {e: [] for e in self.ENGINES}

    def emit(self, engine, fn, waits=(), inc=None, inc_k=1):
        ws = [w for w in waits if w is not None]
        post = None
        if inc is not None:
            inc.n += inc_k
            post = (inc, inc.n)
        self.ops[engine].append((fn, ws, None if inc is None else (inc, inc_k)))
        return post

    def lower(self, block):
        def mk(engine):
            def run(eng):
                hi = {}
                for fn, waits, inc in self.ops[engine]:
                    for (s, v) in waits:
                        if not s.dma and s.owner == engine:
                            continue
                        if hi.get(s.name, -1) >= v:
                            continue
                        eng.wait_ge(s.h, v)
                        hi[s.name] = v
                    instr = fn(eng)
                    if inc is not None:
                        s, k = inc
                        instr.then_inc(s.h, k)
            return run
        block.sync(mk("sync"))
        block.vector(mk("vector"))
        block.scalar(mk("scalar"))
        block.tensor(mk("tensor"))
        block.gpsimd(mk("gpsimd"))


def build_program():
    nc = bass.Bass("TRN2", target_bir_lowering=False,
                   detect_race_conditions=False)
    import contextlib
    nc._sem_cm = contextlib.ExitStack()

    xT = nc.dram_tensor("xT", [D, NR], F32, kind="ExternalInput").ap()
    wqkvT = nc.dram_tensor("wqkvT", [D, 3 * D], F32, kind="ExternalInput").ap()
    wT = {}
    for nm in ("wf", "wi", "wc", "uf", "ui", "uc", "wo", "uo", "wp"):
        wT[nm] = nc.dram_tensor(nm + "T", [D, D], F32, kind="ExternalInput").ap()
    bias_in = {}
    for nm in ("bf", "bi", "bc", "bo", "bp", "inc"):
        bias_in[nm] = nc.dram_tensor(nm + "_t", [128, 8], F32, kind="ExternalInput").ap()
    out_d = nc.dram_tensor("out", [D, NR], F32, kind="ExternalOutput").ap()

    cc_in_k = nc.dram_tensor("cc_in_k", [D, NR], BF16, kind="Internal")
    cc_in_v = nc.dram_tensor("cc_in_v", [NR, D], BF16, kind="Internal")
    cc_out_k = nc.dram_tensor("cc_out_k", [NCORES * D, NR], BF16, kind="Internal")
    cc_out_v = nc.dram_tensor("cc_out_v", [NCORES * NR, D], BF16, kind="Internal")
    bar_in = nc.dram_tensor("bar_in", [128, 1], F32, kind="Internal")
    bar_out = nc.dram_tensor("bar_out", [128, 1], F32, kind="Internal")
    q_sc = nc.dram_tensor("q_sc", [D, NR], BF16, kind="Internal")
    o_sc = nc.dram_tensor("o_sc", [D, NR], BF16, kind="Internal")

    SB_BASE = 16512
    def sb(name, shape, dtype, off):
        return nc.alloc_sbuf_tensor_at(name, shape, dtype, offset=SB_BASE + off)

    xn = sb("xn", [128, DC * NR], BF16, OFF_XN)
    w_rec = {
        "wf": sb("wf_b", [128, 8192], BF16, OFF_WF),
        "wi": sb("wi_b", [128, 8192], BF16, OFF_WI),
        "wc": sb("wc_b", [128, 8192], BF16, OFF_WC),
        "uf": sb("uf_b", [128, 8192], BF16, OFF_UF),
        "ui": sb("ui_b", [128, 8192], BF16, OFF_UI),
        "uc": sb("uc_b", [128, 8192], BF16, OFF_UC),
    }
    wo_b = sb("wo_b", [128, 8192], BF16, OFF_WO)
    uo_b = sb("uo_b", [128, 8192], BF16, OFF_UO)
    wp_b = sb("wp_b", [128, 8192], BF16, OFF_WP)
    cpart = sb("cpart", [128, 2048], F32, OFF_CPART)
    ostream = [sb(f"ostr{i}", [128, 4096], BF16, OFF_OSTREAM + i * 8 * KB)
               for i in range(2)]
    eg = [sb(f"eg{i}", [128, 512], F32, OFF_EG + i * 2 * KB) for i in range(7)]
    egx = [eg[5], sb("egx1", [128, 512], F32, OFF_EG + 14 * KB)]
    c32 = sb("c32", [128, 2048], F32, OFF_C32)
    c16 = sb("c16", [128, 2048], BF16, OFF_C16)
    tfull = sb("tf_raw", [128, 2048], F32, OFF_FG)
    tifull = sb("ti_raw", [128, 2048], F32, OFF_IG)
    grec = [sb(f"grec{i}", [128, 256], F32, OFF_GR + i * KB) for i in range(4)]
    q_ring = [sb(f"qr{i}", [128, 2048], BF16, A_Q + i * 4 * KB) for i in range(2)]
    k_ring = [sb(f"kr{i}", [128, 2048], BF16, A_K + i * 4 * KB) for i in range(2)]
    v_ring = [sb(f"vr{i}", [128, 2080], BF16, A_V + i * 4160) for i in range(2)]
    e_sl = sb("e_sl", [128, 2048], BF16, A_E)
    zrow2 = sb("zrow2", [65, 256], F32, A_ZR)
    zcomp = sb("zcomp", [128, 256], F32, OFF_ATTN + 39 * KB)
    rzb = [sb(f"rzb{i}", [128, 256], F32, OFF_EG + 16 * KB + i * KB)
           for i in range(4)]
    zsrc = [sb(f"zsrc{i}", [65, 256], F32, OFF_EG + 20 * KB + i * KB)
            for i in range(4)]
    ones65f = sb("ones65f", [128, 65], F32, OFF_ATTN + 36 * KB)
    onorm = [sb(f"onorm{i}", [128, 256], BF16, A_ON + i * 512) for i in range(2)]
    ones65 = sb("ones65", [1, 65], F32, A_ONES)
    onesD = sb("onesD", [128, 1], BF16, A_ONES + 512)
    ones1x = sb("ones1x", [1, 128], F32, A_ONES + 576)
    lx = [sb(f"lx{i}", [128, 2048], F32, L_X + i * 8 * KB) for i in range(2)]
    lxb = sb("lxb", [128, 2048], BF16, L_XB)
    lxq = sb("lxq", [128, 2048], BF16, L_XQ)
    lw = [sb(f"lw{i}", [128, 256], F32, L_W + i * KB) for i in range(6)]
    lrow = [sb(f"lrow{i}", [1, 256], F32, L_ROW + i * KB) for i in range(2)]
    qws = [sb(f"qws{i}", [128, 1024], F32, Q_WS + i * 4 * KB) for i in range(2)]
    qwb = [sb(f"qwb{i}", [128, 1024], BF16, Q_WB + i * 2 * KB) for i in range(2)]
    qwv = sb("qwv", [128, 8192], BF16, Q_WV)
    qst = ([sb(f"qst{i}", [128, 512], BF16, Q_ST + i * KB) for i in range(4)]
           + [sb(f"qst{i+4}", [128, 512], BF16, OFF_ATTN + (37 + i) * KB)
              for i in range(2)])
    rws = [sb(f"rws{i}", [128, 2048], F32, RW_ST + i * 8 * KB) for i in range(2)]
    bias_t = {nm: sb(nm + "_sb", [128, 8], F32, OFF_BIAS + j * 32)
              for j, nm in enumerate(("bf", "bi", "bc", "bo", "bp", "inc",
                                      "bfh", "bih"))}

    pbA = nc.alloc_psum_tensor("pbA", [128, 1024], F32)
    pbB = nc.alloc_psum_tensor("pbB", [128, 1024], F32)
    pb = [None, None, None, None] + [
        nc.alloc_psum_tensor(f"pb{i}", [128, 512], F32) for i in range(4, 8)]

    P = Prog(nc)
    s_x = Sem(nc, "s_x", True, "sync")
    s_wr = Sem(nc, "s_wr", True, "sync")
    s_st = Sem(nc, "s_st", True, "sync")
    s_eg = Sem(nc, "s_eg", True, "sync")
    s_out = Sem(nc, "s_out", True, "sync")
    s_g = Sem(nc, "s_g", True, "gpsimd")
    s_cb = Sem(nc, "s_cb", True, "gpsimd")
    s_ck = Sem(nc, "s_ck", True, "gpsimd")
    s_cv = Sem(nc, "s_cv", True, "gpsimd")
    s_bias = Sem(nc, "s_bias", True, "sync")
    sm_lx = [Sem(nc, f"s_lx{i}", True, "sync") for i in range(2)]
    sm_ws = [Sem(nc, f"s_ws{i}", True, "sync") for i in range(8)]
    sm_rw = [Sem(nc, f"s_rw{i}", True, "sync") for i in range(4)]
    sm_st = [Sem(nc, f"s_stg{i}", True, "gpsimd") for i in range(6)]
    sm_st2 = [Sem(nc, f"s_sth{i}", True, "sync") for i in range(6)]
    sm_kr = [Sem(nc, f"s_kr{i}", True, "sync") for i in range(2)]
    sm_vr = [Sem(nc, f"s_vr{i}", True, "sync") for i in range(2)]
    sm_qr = [Sem(nc, f"s_qr{i}", True, "sync") for i in range(2)]
    sm_on = [Sem(nc, f"s_on{i}", True, "gpsimd") for i in range(2)]
    sm_zc = [Sem(nc, f"s_zc{i}", True, "gpsimd") for i in range(2)]
    sm_zs = [Sem(nc, f"s_zs{i}", True, "gpsimd") for i in range(4)]
    sm_egw = [Sem(nc, f"s_egw{i}", True, "sync") for i in range(2)]
    sm_os = [Sem(nc, f"s_os{i}", True, "sync") for i in range(2)]
    s_xe = Sem(nc, "s_xe", True, "sync")
    s_pe = Sem(nc, "s_pe", False, "tensor")
    s_ac = Sem(nc, "s_ac", False, "scalar")
    s_dv = Sem(nc, "s_dv", False, "vector")
    s_pl = Sem(nc, "s_pl", False, "gpsimd")

    E = P.emit

    ev_bias = None
    for nm in ("bf", "bi", "bc", "bo", "bp", "inc"):
        ev_bias = E("sync", lambda eng, nm=nm: eng.dma_start(
            out=bias_t[nm][:, :], in_=bias_in[nm]), inc=s_bias, inc_k=16)
    ev_onesD = E("vector", lambda eng: eng.memset(onesD[:, :], 1.0), inc=s_dv)
    E("vector", lambda eng: eng.memset(ones65f[:, :], 1.0))
    ev_ones1x = E("vector", lambda eng: eng.memset(ones1x[:, :], 1.0), inc=s_dv)
    ev_ones = None
    ev_bh = [None, None]
    ev_bh[0] = E("vector", lambda eng: eng.tensor_scalar_mul(
        bias_t["bfh"][:, :], bias_t["bf"][:, :], 0.5), waits=[ev_bias], inc=s_dv)
    ev_bh[1] = E("vector", lambda eng: eng.tensor_scalar_mul(
        bias_t["bih"][:, :], bias_t["bi"][:, :], 0.5), waits=[ev_bias], inc=s_dv)
    ev_cz = E("vector", lambda eng: eng.memset(c32[:, :], 0.0), inc=s_dv)
    ev_c0 = None
    for kc in range(DC):
        ev_c0 = E("scalar", lambda eng, kc=kc: eng.activation(
            c32[:, kc * 256:(kc + 1) * 256], c32[:, kc * 256:(kc + 1) * 256],
            AF.Identity, scale=f32(0.0), bias=bias_t["inc"][:, kc:kc + 1]),
            waits=[ev_bias, ev_cz], inc=s_ac)
    ev_c16_0 = E("vector", lambda eng: eng.tensor_copy(c16[:, :], c32[:, :]),
                 waits=[ev_c0], inc=s_dv)

    rw_names = ("wf", "uf", "wi", "ui", "wc", "uc")
    rw_cast_evt = {}
    rw_cast_by_slot = {}
    rw_slot = [0]

    def emit_rw_slice():
        slot = rw_slot[0]
        if slot >= 24:
            return
        rw_slot[0] += 1
        nm = rw_names[slot // 4]
        cs = slot % 4
        cur = slot % 2
        waits = []
        if slot >= 2:
            waits.append(rw_cast_by_slot[slot - 2])
        ev_d = E("sync", lambda eng, nm=nm, cs=cs, cur=cur: eng.dma_start(
            out=rws[cur][:, :],
            in_=wT[nm][:, cs * 256:(cs + 1) * 256].rearrange(
                "(kc p) n -> p kc n", kc=DC)),
            waits=waits, inc=sm_rw[cur], inc_k=16)
        ev_c = E("scalar", lambda eng, nm=nm, cs=cs, cur=cur: eng.copy(
            w_rec[nm][:, :].rearrange("p (kc m) -> p kc m", kc=DC)[
                :, :, cs * 256:(cs + 1) * 256],
            rws[cur][:, :].rearrange("p (kc n) -> p kc n", kc=DC)),
            waits=[ev_d], inc=s_ac)
        rw_cast_by_slot[slot] = ev_c
        if cs == 3:
            rw_cast_evt[nm] = ev_c

    rw_all_box = [None]

    lx_ev = [None] * T
    prev = {}
    for s in range(min(2, T)):
        lx_ev[s] = E("sync", lambda eng, s=s: eng.dma_start(
            out=lx[s % 2][:, :],
            in_=xT[:, s * 256:(s + 1) * 256].rearrange("(c p) n -> p c n", c=DC)),
            inc=sm_lx[s % 2], inc_k=16)

    ev_ln_done = None
    pending_apply = [None]
    for s in range(T):
        xs = lx[s % 2]
        ev_cb2 = E("vector", lambda eng, xs=xs: eng.tensor_copy(lxb[:, :], xs[:, :]),
                   waits=[lx_ev[s], prev.get("s1")], inc=s_dv)
        ev_sq = E("vector", lambda eng, xs=xs: eng.tensor_tensor(
            lxq[:, :], xs[:, :], xs[:, :], ALU.mult),
            waits=[prev.get("s2")], inc=s_dv)
        if pending_apply[0] is not None:
            pending_apply[0]()
        ev_s1 = ev_s2 = None
        for c in range(DC):
            ev_s1 = E("tensor", lambda eng, c=c: eng.matmul(
                pbA[0:1, 0:256], onesD[:, :], lxb[:, c * 256:(c + 1) * 256],
                start=(c == 0), stop=(c == DC - 1)),
                waits=[ev_cb2, ev_onesD, prev.get("r1")] if c == 0 else [], inc=s_pe)
        for c in range(DC):
            ev_s2 = E("tensor", lambda eng, c=c: eng.matmul(
                pbA[0:1, 512:768], onesD[:, :], lxq[:, c * 256:(c + 1) * 256],
                start=(c == 0), stop=(c == DC - 1)),
                waits=[ev_sq, prev.get("r2")] if c == 0 else [], inc=s_pe)
        ev_r1 = E("scalar", lambda eng: eng.copy(lrow[0][:, :], pbA[0:1, 0:256]),
                  waits=[ev_s1, prev.get("b1")], inc=s_ac)
        ev_r2 = E("scalar", lambda eng: eng.copy(lrow[1][:, :], pbA[0:1, 512:768]),
                  waits=[ev_s2, prev.get("b2")], inc=s_ac)
        ev_b1 = E("tensor", lambda eng: eng.matmul(
            pbB[:, 0:256], ones1x[:, :], lrow[0][:, :], start=True, stop=True),
            waits=[ev_r1, prev.get("m")], inc=s_pe)
        ev_b2 = E("tensor", lambda eng: eng.matmul(
            pbB[:, 512:768], ones1x[:, :], lrow[1][:, :], start=True, stop=True),
            waits=[ev_r2, prev.get("mq")], inc=s_pe)
        ev_m = E("vector", lambda eng: eng.tensor_scalar_mul(
            lw[0][:, :], pbB[:, 0:256], 1.0 / D), waits=[ev_b1], inc=s_dv)
        ev_mq = E("vector", lambda eng: eng.tensor_scalar_mul(
            lw[1][:, :], pbB[:, 512:768], 1.0 / D), waits=[ev_b2], inc=s_dv)
        E("vector", lambda eng: eng.tensor_tensor(
            lw[2][:, :], lw[0][:, :], lw[0][:, :], ALU.mult), inc=s_dv)
        ev_var = E("vector", lambda eng: eng.tensor_tensor(
            lw[3][:, :], lw[1][:, :], lw[2][:, :], ALU.subtract),
            waits=[prev.get("ln")], inc=s_dv)
        ev_ln = E("scalar", lambda eng: eng.activation(
            lw[4][:, :], lw[3][:, :], AF.Ln, bias=f32(1e-5)),
            waits=[ev_var], inc=s_ac)
        ev_al = E("scalar", lambda eng: eng.activation(
            lw[5][:, :], lw[4][:, :], AF.Exp, scale=f32(-0.5)), inc=s_ac)
        def mk_apply(s=s, xs=xs, ev_al=ev_al, ev_m=ev_m):
            def go():
                ev_ap = None
                for c in range(DC):
                    E("vector", lambda eng, xs=xs, c=c: eng.tensor_tensor(
                        lw[2][:, :], xs[:, c * 256:(c + 1) * 256], lw[0][:, :],
                        ALU.subtract), waits=[ev_al, ev_m], inc=s_dv)
                    ev_ap = E("vector", lambda eng, s=s, c=c: eng.tensor_tensor(
                        xn[:, c * 2048 + s * 256: c * 2048 + (s + 1) * 256],
                        lw[2][:, :], lw[5][:, :], ALU.mult), inc=s_dv)
                if s + 2 < T:
                    lx_ev[s + 2] = E("sync", lambda eng, s=s: eng.dma_start(
                        out=lx[s % 2][:, :],
                        in_=xT[:, (s + 2) * 256:(s + 3) * 256].rearrange(
                            "(c p) n -> p c n", c=DC)),
                        waits=[ev_ap], inc=sm_lx[s % 2], inc_k=16)
                return ev_ap
            return go
        pending_apply[0] = mk_apply()
        prev = {"s1": ev_s1, "s2": ev_s2, "r1": ev_r1, "r2": ev_r2,
                "b1": ev_b1, "b2": ev_b2, "m": ev_m, "mq": ev_mq, "ln": ev_ln}
        emit_rw_slice()
        emit_rw_slice()
        emit_rw_slice()
    ev_ln_done = pending_apply[0]()


    rec_units = []
    rec_last_cmm = {}
    rec_cast_evt = {-1: ev_c16_0}
    rec_state = {"bank": 0, "evac": {}, "c_upd": {}}

    def mk_rec_unit(t, gi, mc):
        gamma = ("f", "i", "g")[gi]
        wname = {"f": "wf", "i": "wi", "g": "wc"}[gamma]
        uname = {"f": "uf", "i": "ui", "g": "uc"}[gamma]

        def unit():
            k = rec_state["bank"]
            bank = pb[6 + k % 2]
            rec_state["bank"] = k + 1
            waits0 = [rw_all_box[0], ev_ln_done,
                      rec_cast_evt.get(t - 1), rec_state["evac"].get(k - 2)]
            mmev = None
            for kc in range(DC):
                mmev = E("tensor", lambda eng, kc=kc, bank=bank: eng.matmul(
                    bank[:, 0:256],
                    w_rec[wname][:, kc * 1024 + mc * 128: kc * 1024 + (mc + 1) * 128],
                    xn[:, kc * 2048 + t * 256: kc * 2048 + (t + 1) * 256],
                    start=(kc == 0), stop=False),
                    waits=waits0 if kc == 0 else [], inc=s_pe)
            for kc in range(DC):
                mmev = E("tensor", lambda eng, kc=kc, bank=bank: eng.matmul(
                    bank[:, 0:256],
                    w_rec[uname][:, kc * 1024 + mc * 128: kc * 1024 + (mc + 1) * 128],
                    c16[:, kc * 256:(kc + 1) * 256],
                    start=False, stop=(kc == DC - 1)), inc=s_pe)
            if gamma == "g":
                rec_last_cmm[t] = mmev
            if gamma == "f":
                ev_e = E("scalar", lambda eng, bank=bank: eng.activation(
                    tfull[:, mc * 256:(mc + 1) * 256], bank[:, 0:256], AF.Tanh,
                    scale=f32(0.5), bias=bias_t["bfh"][:, mc:mc + 1]),
                    waits=[mmev, ev_bh[0]], inc=s_ac)
            elif gamma == "i":
                ev_e = E("scalar", lambda eng, bank=bank: eng.activation(
                    tifull[:, mc * 256:(mc + 1) * 256], bank[:, 0:256], AF.Tanh,
                    scale=f32(0.5), bias=bias_t["bih"][:, mc:mc + 1]),
                    waits=[mmev, ev_bh[1]], inc=s_ac)
            else:
                gslot = grec[mc % 2]
                ev_e = E("scalar", lambda eng, bank=bank, gslot=gslot: eng.activation(
                    gslot[:, :], bank[:, 0:256], AF.Tanh,
                    scale=f32(1.0), bias=bias_t["bc"][:, mc:mc + 1]),
                    waits=[mmev, rec_state["c_upd"].get((t, mc - 2))], inc=s_ac)
                t1, t2 = grec[2], grec[3]
                E("vector", lambda eng: eng.scalar_tensor_tensor(
                    t1[:, :], tfull[:, mc * 256:(mc + 1) * 256], 1.0,
                    c32[:, mc * 256:(mc + 1) * 256], ALU.add, ALU.mult),
                    waits=[ev_e], inc=s_dv)
                ev_t2 = E("vector", lambda eng, gslot=gslot: eng.scalar_tensor_tensor(
                    t2[:, :], tifull[:, mc * 256:(mc + 1) * 256], 1.0,
                    gslot[:, :], ALU.add, ALU.mult), inc=s_dv)
                rec_state["c_upd"][(t, mc)] = ev_t2
                E("vector", lambda eng: eng.tensor_tensor(
                    t1[:, :], t1[:, :], t2[:, :], ALU.add), inc=s_dv)
                E("vector", lambda eng: eng.tensor_scalar_mul(
                    c32[:, mc * 256:(mc + 1) * 256], t1[:, :], 0.5), inc=s_dv)
                if mc == DC - 1:
                    rec_cast_evt[t] = E("vector", lambda eng: eng.tensor_copy(
                        c16[:, :], c32[:, :]),
                        waits=[rec_last_cmm[t]], inc=s_dv)
            rec_state["evac"][k] = ev_e
        return unit

    for t in range(T):
        for gi in range(3):
            for mc in range(DC):
                rec_units.append(mk_rec_unit(t, gi, mc))
    ri = [0]

    def pump_rec(nun):
        for _ in range(nun):
            if ri[0] < len(rec_units):
                rec_units[ri[0]]()
                ri[0] += 1

    rw_all_box[0] = rw_cast_by_slot[23]

    ms_cast = {}
    ms_dma = {}
    ms_mm_last = {}
    st_free = {}
    evac_evts = []
    st_i = [0]
    kv_dma_last = [None]
    qsc_dma_last = [None]
    qwb_prev = {}
    qkv_group = [0]

    ms_order = list(range(8, 24)) + list(range(0, 8))
    seq_of = {ms: i for i, ms in enumerate(ms_order)}

    def wslice_load(seq):
        ms = ms_order[seq]
        widx = seq % 2
        if seq >= 2:
            waits = [ms_cast[ms_order[seq - 2]]]
        else:
            waits = [ev_ln_done]
        ms_dma[ms] = E("sync", lambda eng, ms=ms, widx=widx: eng.dma_start(
            out=qws[widx][:, :],
            in_=wqkvT[:, ms * 128:(ms + 1) * 128].rearrange(
                "(kc p) n -> p kc n", kc=DC)),
            waits=waits, inc=sm_ws[widx], inc_k=16)

    def wslice_cast(seq):
        ms = ms_order[seq]
        widx = seq % 2
        waits = [ms_dma[ms]]
        if ms < 16:
            waits.append(ms_mm_last.get(qwb_prev.get(seq % 2)))
            qwb_prev[seq % 2] = ms
            if seq >= 16:
                ev = E("scalar", lambda eng, widx=widx, seq=seq: eng.copy(
                    qwb[seq % 2][:, :], qws[widx][:, :]), waits=waits, inc=s_ac)
            else:
                ev = E("vector", lambda eng, widx=widx, seq=seq: eng.tensor_copy(
                    qwb[seq % 2][:, :], qws[widx][:, :]), waits=waits, inc=s_dv)
        else:
            ev = E("vector", lambda eng, ms=ms, widx=widx: eng.tensor_copy(
                qwv[:, :].rearrange("p (kc m) -> p kc m", kc=DC)[
                    :, :, (ms - 16) * 128:(ms - 15) * 128],
                qws[widx][:, :].rearrange("p (kc n) -> p kc n", kc=DC)),
                waits=waits, inc=s_dv)
        ms_cast[ms] = ev

    def qkv_qk_group(ms, nck):
        bank = pb[4 + (qkv_group[0] % 2)]
        qkv_group[0] += 1
        mmev = None
        for kc in range(DC):
            w0 = []
            if kc == 0:
                w0 = [ms_cast[ms]]
                if len(evac_evts) >= 2:
                    w0.append(evac_evts[-2])
            mmev = E("tensor", lambda eng, ms=ms, nck=nck, kc=kc, bank=bank:
                     eng.matmul(
                         bank[:, :],
                         qwb[seq_of[ms] % 2][:, kc * 128:(kc + 1) * 128],
                         xn[:, kc * 2048 + nck * 512: kc * 2048 + (nck + 1) * 512],
                         start=(kc == 0), stop=(kc == DC - 1)),
                     waits=w0, inc=s_pe)
        ms_mm_last[ms] = mmev
        sl = st_i[0] % 6
        st_i[0] += 1
        ev_e = E("scalar", lambda eng, sl=sl, bank=bank: eng.copy(
            qst[sl][:, :], bank[:, :]),
            waits=[mmev, st_free.get(sl)], inc=s_ac)
        evac_evts.append(ev_e)
        if ms < 8:
            ev_w = E("sync", lambda eng, ms=ms, nck=nck, sl=sl: eng.dma_start(
                out=q_sc.ap()[ms * 128:(ms + 1) * 128, nck * 512:(nck + 1) * 512],
                in_=qst[sl][:, :]), waits=[ev_e], inc=sm_st2[sl], inc_k=16)
            qsc_dma_last[0] = ev_w
        else:
            ev_w = E("gpsimd", lambda eng, ms=ms, nck=nck, sl=sl: eng.dma_start(
                out=cc_in_k.ap()[(ms - 8) * 128:(ms - 7) * 128,
                                 nck * 512:(nck + 1) * 512],
                in_=qst[sl][:, :]), waits=[ev_e], inc=sm_st[sl], inc_k=16)
            kv_dma_last[0] = ev_w
        st_free[sl] = ev_w

    def qkv_v_group(rt, dvc):
        bank = pb[4 + (qkv_group[0] % 2)]
        qkv_group[0] += 1
        mmev = None
        for kc in range(DC):
            w0 = []
            if kc == 0:
                w0 = [ms_cast[23]]
                if len(evac_evts) >= 2:
                    w0.append(evac_evts[-2])
            mmev = E("tensor", lambda eng, rt=rt, dvc=dvc, kc=kc, bank=bank:
                     eng.matmul(
                         bank[:, :],
                         xn[:, kc * 2048 + rt * 128: kc * 2048 + (rt + 1) * 128],
                         qwv[:, kc * 1024 + dvc * 512: kc * 1024 + (dvc + 1) * 512],
                         start=(kc == 0), stop=(kc == DC - 1)),
                     waits=w0, inc=s_pe)
        sl = st_i[0] % 6
        st_i[0] += 1
        ev_e = E("scalar", lambda eng, sl=sl, bank=bank: eng.copy(
            qst[sl][:, :], bank[:, :]),
            waits=[mmev, st_free.get(sl)], inc=s_ac)
        evac_evts.append(ev_e)
        ev_w = E("sync", lambda eng, rt=rt, dvc=dvc, sl=sl: eng.dma_start(
            out=cc_in_v.ap()[rt * 128:(rt + 1) * 128, dvc * 512:(dvc + 1) * 512],
            in_=qst[sl][:, :]), waits=[ev_e], inc=sm_st2[sl], inc_k=16)
        st_free[sl] = ev_w
        kv_dma_last[0] = ev_w
        v_mm_last[0] = mmev

    v_mm_last = [None]
    wslice_load(0)
    wslice_load(1)
    for seq in range(8):
        wslice_cast(seq)
        if seq + 2 < 24:
            wslice_load(seq + 2)
        for nck in range(4):
            qkv_qk_group(ms_order[seq], nck)
    for seq in range(8, 16):
        wslice_cast(seq)
        if seq + 2 < 24:
            wslice_load(seq + 2)
    E("gpsimd", lambda eng: eng.collective_compute(
        "AllGather", ALU.bypass, replica_groups=[list(range(NCORES))],
        ins=[cc_in_k.ap().opt()], outs=[cc_out_k.ap().opt()]),
        waits=[st_free.get(i) for i in range(6)],
        inc=s_ck, inc_k=1)
    for rt in range(16):
        for dvc in range(2):
            qkv_v_group(rt, dvc)
            pump_rec(3)

    ev_ones = None
    for r in range(2):
        for col in (0, 129):
            ev_ones = E("vector", lambda eng, r=r, col=col: eng.memset(
                v_ring[r][:, :].rearrange("p (j q) -> p j q", j=JT)[
                    :, :, col:col + 1], 1.0),
                waits=[v_mm_last[0]], inc=s_dv)

    E("gpsimd", lambda eng: eng.collective_compute(
        "AllGather", ALU.bypass, replica_groups=[list(range(NCORES))],
        ins=[cc_in_v.ap().opt()], outs=[cc_out_v.ap().opt()]),
        waits=[st_free.get(i) for i in range(6)],
        inc=s_cv, inc_k=1)

    pump_rec(32)

    for seq in range(16, 24):
        wslice_cast(seq)
        if seq + 2 < 24:
            wslice_load(seq + 2)
        for nck in range(4):
            qkv_qk_group(ms_order[seq], nck)

    qkv_mm_done = [ms_mm_last[ms_order[23]]]
    qkv_cast_done = [ms_cast[ms_order[23]]]

    attn_units = []
    att_state = {"dma": {}, "pe_done": {}, "norm": {}, "q_dma": {},
                 "q_used": {}, "osc": [None], "on0": None, "on1": None,
                 "zd": {}}

    def emit_attn_dma(n):
        if n >= 64:
            return
        b, pr = divmod(n, PAIRS)
        r = n % 2
        waits = [(s_ck, 1), att_state["pe_done"].get(n - 2)]
        waits_v = [(s_cv, 1), att_state["pe_done"].get(n - 2)]
        if n < 2:
            waits += [qkv_mm_done[0], qkv_cast_done[0]]
            waits_v += [qkv_mm_done[0], qkv_cast_done[0]]
        if pr == 0:
            att_state["q_dma"][b] = E("sync", lambda eng, b=b: eng.dma_start(
                out=q_ring[b % 2][:, :].rearrange("p (c i) -> p c i", c=DC),
                in_=q_sc.ap().rearrange("(c p) (t i) -> p c t i",
                                        c=DC, t=T)[:, :, b, :]),
                waits=[st_free.get(i) for i in range(6)]
                      + [att_state["q_used"].get(b - 2), qkv_mm_done[0]],
                inc=sm_qr[b % 2 if True else 0], inc_k=16)
        ev_k = E("sync", lambda eng, b=b, pr=pr, r=r: eng.dma_start(
            out=k_ring[r][:, :].rearrange("p (rr i) -> p rr i", rr=NCORES),
            in_=cc_out_k.ap().rearrange("(rr p) (t i) -> p rr t i",
                                        rr=NCORES, t=T)[
                pr * 128:(pr + 1) * 128, :, b, :]),
            waits=waits, inc=sm_kr[r], inc_k=16)
        ev_v = None
        for ih in range(2):
            ev_v = E("sync", lambda eng, b=b, pr=pr, r=r, ih=ih: eng.dma_start(
                out=v_ring[r][:, :].rearrange("p (rr ih q) -> p rr ih q",
                                              rr=NCORES, ih=2)[:, :, ih, 1:129],
                in_=cc_out_v.ap().rearrange("(rr t ih p) c -> t ih p rr c",
                                            rr=NCORES, t=T, ih=2)[
                    b, ih][:, :, pr * 128:(pr + 1) * 128]),
                waits=waits_v, inc=sm_vr[r], inc_k=16)
        att_state["dma"][n] = (ev_k, ev_v)

    def mk_attn_unit(n):
        b, pr = divmod(n, PAIRS)
        r = n % 2

        def unit():
            ev_k, ev_v = att_state["dma"][n]
            ev_q = att_state["q_dma"][b]
            kt, vt, qt = k_ring[r], v_ring[r], q_ring[b % 2]
            exp_evts = {}
            pv_evts = {}
            mm_o = None
            prev_na = att_state.get("norm_a")
            prev_norm = att_state["norm"].get(n - 1)

            def emit_S(jt):
                sT = pbA if jt % 2 == 0 else pbB
                cb = 0
                w_s = [ev_k, ev_q, exp_evts.get(jt - 2)]
                E("tensor", lambda eng, jt=jt, sT=sT, cb=cb: eng.matmul(
                    sT[:, cb:cb + 256], kt[0:64, jt * 128:(jt + 1) * 128],
                    qt[0:64, pr * 256:(pr + 1) * 256], start=True, stop=True),
                    waits=w_s, inc=s_pe)
                ev_sb = E("tensor", lambda eng, jt=jt, sT=sT, cb=cb: eng.matmul(
                    sT[:, 512 + cb:768 + cb],
                    kt[64:128, jt * 128:(jt + 1) * 128],
                    qt[64:128, pr * 256:(pr + 1) * 256], start=True, stop=True),
                    inc=s_pe)
                esl = jt % 4
                ev_x = E("scalar", lambda eng, sT=sT, esl=esl, cb=cb:
                         eng.activation(
                             e_sl[:, esl * 512:(esl + 1) * 512].rearrange(
                                 "p (g c) -> p g c", g=2),
                             sT[:, :].rearrange("p (g c) -> p g c", g=2)[
                                 :, :, cb:cb + 256],
                             AF.Exp, scale=f32(0.125), bias=f32(EXP_SHIFT)),
                         waits=[ev_sb, pv_evts.get(jt - 4)], inc=s_ac)
                exp_evts[jt] = ev_x

            def emit_PV(jt):
                nonlocal mm_o
                esl = jt % 4
                w_pv = [exp_evts[jt], ev_v, ev_ones]
                w_pvb = []
                if jt == 0:
                    w_pv.append(prev_na)
                    w_pvb = [prev_norm]
                E("tensor", lambda eng, jt=jt, esl=esl: eng.matmul(
                    pb[4][0:65, 0:256], vt[:, jt * 130: jt * 130 + 65],
                    e_sl[:, esl * 512: esl * 512 + 256],
                    start=(jt == 0), stop=(jt == JT - 1)), waits=w_pv, inc=s_pe)
                mm_o = E("tensor", lambda eng, jt=jt, esl=esl: eng.matmul(
                    pb[5][0:65, 0:256], vt[:, jt * 130 + 65: jt * 130 + 130],
                    e_sl[:, esl * 512 + 256: esl * 512 + 512],
                    start=(jt == 0), stop=(jt == JT - 1)), waits=w_pvb, inc=s_pe)
                pv_evts[jt] = mm_o

            emit_S(0)
            emit_S(1)
            for jt in range(2, JT):
                emit_S(jt)
                emit_PV(jt - 2)
            emit_PV(JT - 2)
            emit_PV(JT - 1)
            att_state["pe_done"][n] = mm_o
            if pr == PAIRS - 1:
                att_state["q_used"][b] = mm_o
            zda, zdb = att_state["zd"].get(n - 1, (None, None))
            ev_na = E("vector", lambda eng: eng.tensor_copy(
                onorm[0][0:65, :], pb[4][0:65, 0:256]),
                waits=[mm_o, att_state["on0"]], inc=s_dv)
            ev_za = E("vector", lambda eng: eng.tensor_copy(
                zrow2[0:1, :], pb[4][0:1, 0:256]), waits=[zda], inc=s_dv)
            ev_nb = E("vector", lambda eng: eng.tensor_copy(
                onorm[1][0:64, :], pb[5][0:64, 0:256]),
                waits=[att_state["on1"]], inc=s_dv)
            ev_zb = E("vector", lambda eng: eng.tensor_copy(
                zrow2[64:65, :], pb[5][64:65, 0:256]), waits=[zdb], inc=s_dv)
            att_state["norm_a"] = ev_za
            att_state["norm"][n] = ev_zb
            ev_wa = E("gpsimd", lambda eng, b=b, pr=pr: eng.dma_start(
                out=o_sc.ap()[pr * 128: pr * 128 + 64, b * 256:(b + 1) * 256],
                in_=onorm[0][1:65, :]), waits=[ev_na], inc=sm_on[0], inc_k=16)
            ev_wb = E("gpsimd", lambda eng, b=b, pr=pr: eng.dma_start(
                out=o_sc.ap()[pr * 128 + 64: pr * 128 + 128, b * 256:(b + 1) * 256],
                in_=onorm[1][0:64, :]), waits=[ev_nb], inc=sm_on[1], inc_k=16)
            ev_da = E("gpsimd", lambda eng, n=n: eng.dma_start(
                out=zcomp[2 * n:2 * n + 1, :],
                in_=zrow2[0:1, :]), waits=[ev_za], inc=sm_zc[0], inc_k=16)
            ev_db = E("gpsimd", lambda eng, n=n: eng.dma_start(
                out=zcomp[2 * n + 1:2 * n + 2, :],
                in_=zrow2[64:65, :]), waits=[ev_zb], inc=sm_zc[1], inc_k=16)
            att_state["zd"][n] = (ev_da, ev_db)
            att_state["on0"] = ev_wa
            att_state["on1"] = ev_wb
            att_state["osc"][0] = ev_wb
            emit_attn_dma(n + 2)
        return unit

    for n in range(64):
        attn_units.append(mk_attn_unit(n))

    eg_cast = {}
    eg_stage = {}
    egslot = [0]

    eg_names = [("wo", wo_b), ("uo", uo_b), ("wp", wp_b)]
    eg_dma = {}

    emit_attn_dma(0)
    emit_attn_dma(1)
    eg_emitted = [0]

    def emit_eg_one():
        sl = eg_emitted[0]
        eg_emitted[0] += 1
        if sl >= 12:
            return
        nm, dst = eg_names[sl // 4]
        cs = sl % 4
        cur = sl % 2
        waits = [rec_cast_evt[T - 1], rec_last_cmm[T - 1],
                 eg_stage.get(sl - 2)]
        ev_d = E("sync", lambda eng, nm=nm, cs=cs, cur=cur:
                 eng.dma_start(
                     out=rws[cur][:, :],
                     in_=wT[nm][:, cs * 256:(cs + 1) * 256].rearrange(
                         "(kc p) n -> p kc n", kc=DC)),
                 waits=waits, inc=sm_egw[cur], inc_k=16)
        ev_c = E("vector", lambda eng, dst=dst, cs=cs, cur=cur:
                 eng.tensor_copy(
                     dst[:, :].rearrange("p (kc m) -> p kc m", kc=DC)[
                         :, :, cs * 256:(cs + 1) * 256],
                     rws[cur][:, :].rearrange("p (kc n) -> p kc n", kc=DC)),
                 waits=[ev_d], inc=s_dv)
        eg_stage[sl] = ev_c
        if cs == 3:
            eg_cast[nm] = ev_c

    ev_o_pre = {}
    for n in range(64):
        attn_units[n]()
        pump_rec(2 if n < 32 else 0)
        if 34 <= n < 46:
            emit_eg_one()
        if n == 50 or n == 54:
            qc = 0 if n == 50 else 1
            ev_o_pre[qc] = E("sync", lambda eng, qc=qc: eng.dma_start(
                out=ostream[qc % 2][:, :].rearrange("p (c n) -> p c n", c=DC),
                in_=o_sc.ap()[:, qc * 512:(qc + 1) * 512].rearrange(
                    "(c p) n -> p c n", c=DC)),
                waits=[(sm_on[0], 256 * (qc + 1)), (sm_on[1], 256 * (qc + 1)),
                       rec_cast_evt[T - 1], rec_last_cmm[T - 1]],
                inc=sm_os[qc % 2], inc_k=16)
    pump_rec(len(rec_units))
    ev_rec_done = rec_cast_evt[T - 1]
    ev_rz = E("vector", lambda eng: eng.reciprocal(zcomp[:, :], zcomp[:, :]),
              waits=[(sm_zc[0], sm_zc[0].n), (sm_zc[1], sm_zc[1].n)], inc=s_dv)
    rz_state = {"pbA": None, "pbB": None,
                "used": {}, "ev": {}, "src_used": {}}

    def emit_rz_half(ti, hb, qc, kc, ring, ev_o):
        j = (2 * ti + hb) % 4
        b = 2 * qc + hb
        rA = 2 * (b * 8 + kc)
        rB = rA + 1
        dA = E("gpsimd", lambda eng, j=j, rA=rA: eng.dma_start(
            out=zsrc[j][0:1, :], in_=zcomp[rA:rA + 1, :]),
            waits=[ev_rz, rz_state["src_used"].get(j)],
            inc=sm_zs[j], inc_k=16)
        dB = E("gpsimd", lambda eng, j=j, rB=rB: eng.dma_start(
            out=zsrc[j][64:65, :], in_=zcomp[rB:rB + 1, :]),
            inc=sm_zs[j], inc_k=16)
        bcA = E("tensor", lambda eng, j=j: eng.matmul(
            pb[4][0:64, 0:256], ones65f[0:1, 0:64], zsrc[j][0:1, :],
            start=True, stop=True),
            waits=[dA, dB, rz_state["pbA"], rz_state["used"].get(j)],
            inc=s_pe)
        bcB = E("tensor", lambda eng, j=j: eng.matmul(
            pb[5][64:128, 0:256], ones65f[64:65, 0:64], zsrc[j][64:65, :],
            start=True, stop=True), waits=[rz_state["pbB"]], inc=s_pe)
        evA = E("vector", lambda eng, j=j: eng.tensor_copy(
            rzb[j][0:64, :], pb[4][0:64, 0:256]), waits=[bcB], inc=s_dv)
        evB = E("vector", lambda eng, j=j: eng.tensor_copy(
            rzb[j][64:128, :], pb[5][64:128, 0:256]), inc=s_dv)
        rz_state["pbA"] = evA
        rz_state["pbB"] = evB
        rz_state["src_used"][j] = bcB
        ev_nz = E("vector", lambda eng, j=j, kc=kc, hb=hb, ring=ring:
                  eng.tensor_tensor(
                      ring[:, kc * 512 + hb * 256: kc * 512 + (hb + 1) * 256],
                      ring[:, kc * 512 + hb * 256: kc * 512 + (hb + 1) * 256],
                      rzb[j][:, :], ALU.mult),
                  waits=[evB, ev_o], inc=s_dv)
        rz_state["used"][j] = ev_nz
        return ev_nz
    ev_attn_done = E("gpsimd", lambda eng: eng.nop(),
                     waits=[att_state["on0"], att_state["on1"]], inc=s_pl)

    ev_cp_evac = {}
    ev_cpart = None
    for mc in range(DC):
        bank = pb[6 + mc % 2]
        mmev = None
        for kc in range(DC):
            w0 = [eg_cast["uo"], ev_rec_done, ev_cp_evac.get(mc - 2)] if kc == 0 else []
            mmev = E("tensor", lambda eng, mc=mc, kc=kc, bank=bank: eng.matmul(
                bank[:, 0:256],
                uo_b[:, kc * 1024 + mc * 128: kc * 1024 + (mc + 1) * 128],
                c16[:, kc * 256:(kc + 1) * 256],
                start=(kc == 0), stop=(kc == DC - 1)), waits=w0, inc=s_pe)
        ev_cpart = E("scalar", lambda eng, mc=mc, bank=bank: eng.copy(
            cpart[:, mc * 256:(mc + 1) * 256], bank[:, 0:256]),
            waits=[mmev], inc=s_ac)
        ev_cp_evac[mc] = ev_cpart

    ost_ev = {}
    ring_rd = {}
    egp = {}
    ev_out_last = None
    for qc in range(4):
        ring = ostream[qc % 2]
        if qc in ev_o_pre:
            ev_o = ev_o_pre[qc]
        else:
            ev_o = E("sync", lambda eng, qc=qc, ring=ring: eng.dma_start(
                out=ring[:, :].rearrange("p (c n) -> p c n", c=DC),
                in_=o_sc.ap()[:, qc * 512:(qc + 1) * 512].rearrange(
                    "(c p) n -> p c n", c=DC)),
                waits=[ev_attn_done, rec_last_cmm[T - 1], ring_rd.get(qc - 2)],
                inc=sm_os[qc % 2], inc_k=16)
        ev_nz_last = None
        for kc in range(DC):
            for hb in range(2):
                ev_nz_last = emit_rz_half(qc * 8 + kc, hb, qc, kc, ring, ev_o)
        for mc in range(DC):
            par = mc % 2
            bank_w = pbA[:, par * 512:(par + 1) * 512]
            bank_p = pbB[:, par * 512:(par + 1) * 512]
            xi = (qc * 8 + mc) % 2
            ev_x = E("sync", lambda eng, qc=qc, mc=mc, xi=xi: eng.dma_start(
                out=egx[xi][:, :],
                in_=xT[mc * 128:(mc + 1) * 128, qc * 512:(qc + 1) * 512]),
                waits=[egp.get(("fin", xi), ev_attn_done)], inc=s_xe, inc_k=16)
            mm_w = None
            for kc in range(DC):
                w0 = [eg_cast["wo"], ev_attn_done,
                      egp.get(("t1", par))] if kc == 0 else []
                mm_w = E("tensor", lambda eng, qc=qc, mc=mc, kc=kc, bank_w=bank_w:
                         eng.matmul(
                             bank_w[:, :],
                             wo_b[:, kc * 1024 + mc * 128: kc * 1024 + (mc + 1) * 128],
                             xn[:, kc * 2048 + qc * 512: kc * 2048 + (qc + 1) * 512],
                             start=(kc == 0), stop=(kc == DC - 1)),
                         waits=w0, inc=s_pe)
            mm_p = None
            for kc in range(DC):
                w0 = ([eg_cast["wp"], ev_o, ev_nz_last, egp.get(("s", par))]
                      if kc == 0 else [])
                mm_p = E("tensor", lambda eng, mc=mc, kc=kc, bank_p=bank_p,
                         ring=ring: eng.matmul(
                             bank_p[:, :],
                             wp_b[:, kc * 1024 + mc * 128: kc * 1024 + (mc + 1) * 128],
                             ring[:, kc * 512:(kc + 1) * 512],
                             start=(kc == 0), stop=(kc == DC - 1)),
                         waits=w0, inc=s_pe)
            ev_t1a = E("vector", lambda eng, mc=mc, bank_w=bank_w: eng.tensor_tensor(
                eg[0][:, 0:256], bank_w[:, 0:256], cpart[:, mc * 256:(mc + 1) * 256],
                ALU.add), waits=[mm_w, ev_cpart, egp.get("h")], inc=s_dv)
            ev_t1b = E("vector", lambda eng, mc=mc, bank_w=bank_w: eng.tensor_tensor(
                eg[0][:, 256:512], bank_w[:, 256:512],
                cpart[:, mc * 256:(mc + 1) * 256], ALU.add), inc=s_dv)
            egp[("t1", par)] = ev_t1b
            ev_h = E("scalar", lambda eng, mc=mc: eng.activation(
                eg[1][:, :], eg[0][:, :], AF.Tanh, scale=f32(1.0),
                bias=bias_t["bo"][:, mc:mc + 1]),
                waits=[ev_t1b, egp.get("g")], inc=s_ac)
            egp["h"] = ev_h
            ev_th = E("scalar", lambda eng: eng.activation(
                eg[2][:, :], eg[1][:, :], AF.Tanh, scale=f32(0.5)),
                waits=[egp.get("sgm")], inc=s_ac)
            ev_sg = E("vector", lambda eng: eng.tensor_scalar(
                eg[3][:, :], eg[2][:, :], 0.5, 0.5, ALU.mult, ALU.add),
                waits=[ev_th], inc=s_dv)
            ev_g = E("vector", lambda eng: eng.tensor_tensor(
                eg[2][:, :], eg[3][:, :], eg[1][:, :], ALU.mult), inc=s_dv)
            egp["g"] = ev_g
            ev_s = E("scalar", lambda eng, mc=mc, bank_p=bank_p: eng.activation(
                eg[4][:, :], bank_p[:, :], AF.Identity, scale=f32(1.0),
                bias=bias_t["bp"][:, mc:mc + 1]),
                waits=[mm_p, egp.get("sgm")], inc=s_ac)
            ev_sgm = E("vector", lambda eng: eng.tensor_tensor(
                eg[3][:, :], eg[4][:, :], eg[2][:, :], ALU.mult),
                waits=[ev_s, ev_g], inc=s_dv)
            egp[("s", par)] = ev_sgm
            egp["sgm"] = ev_sgm
            ev_fin = E("vector", lambda eng, xi=xi: eng.tensor_tensor(
                eg[6][:, :], eg[3][:, :], egx[xi][:, :], ALU.add),
                waits=[ev_x, ev_out_last], inc=s_dv)
            egp[("fin", xi)] = ev_fin
            ev_out_last = E("sync", lambda eng, qc=qc, mc=mc: eng.dma_start(
                out=out_d[mc * 128:(mc + 1) * 128, qc * 512:(qc + 1) * 512],
                in_=eg[6][:, :]), waits=[ev_fin], inc=s_out, inc_k=16)
        ost_ev[qc] = ev_out_last
        ring_rd[qc] = mm_p

    E("sync", lambda eng: eng.wait_ge(s_out.h, s_out.n), waits=[(s_out, s_out.n)])

    with nc.Block() as block:
        P.lower(block)
    nc._sem_cm.close()
    return nc


_NC_CACHE = {}


def kernel(L0_attn_x, L0_attn_w_qkv, L0_attn_w_proj, L0_attn_b_proj,
           L0_ffn_wf, L0_ffn_uf, L0_ffn_bf, L0_ffn_wi, L0_ffn_ui, L0_ffn_bi,
           L0_ffn_wc, L0_ffn_uc, L0_ffn_bc, L0_ffn_wo, L0_ffn_uo, L0_ffn_bo,
           L0_ffn_in_c):
    x = np.ascontiguousarray(np.asarray(L0_attn_x, dtype=np.float32))
    cT = lambda a: np.ascontiguousarray(np.asarray(a, dtype=np.float32).T)
    bt = lambda a: np.ascontiguousarray(
        np.asarray(a, dtype=np.float32).reshape(8, 128).T)
    shared = {
        "wqkvT": cT(L0_attn_w_qkv),
        "wfT": cT(L0_ffn_wf), "wiT": cT(L0_ffn_wi), "wcT": cT(L0_ffn_wc),
        "ufT": cT(L0_ffn_uf), "uiT": cT(L0_ffn_ui), "ucT": cT(L0_ffn_uc),
        "woT": cT(L0_ffn_wo), "uoT": cT(L0_ffn_uo), "wpT": cT(L0_attn_w_proj),
        "bf_t": bt(L0_ffn_bf), "bi_t": bt(L0_ffn_bi), "bc_t": bt(L0_ffn_bc),
        "bo_t": bt(L0_ffn_bo), "bp_t": bt(L0_attn_b_proj),
        "inc_t": bt(L0_ffn_in_c),
    }
    in_maps = []
    for j in range(NCORES):
        xTj = np.ascontiguousarray(
            x[:, j * I:(j + 1) * I, :].transpose(2, 0, 1)).reshape(D, NR)
        m = {"xT": xTj}
        m.update(shared)
        in_maps.append(m)

    if "nc" not in _NC_CACHE:
        _NC_CACHE["nc"] = build_program()
    nc = _NC_CACHE["nc"]
    res = run_bass_kernel_spmd(nc, in_maps, core_ids=list(range(NCORES)))
    outs = []
    for j in range(NCORES):
        o = np.asarray(res.results[j]["out"])
        outs.append(o.reshape(D, T, I).transpose(1, 2, 0))
    return np.ascontiguousarray(np.concatenate(outs, axis=1))


# version 35
# speedup vs baseline: 1.0927x; 1.0927x over previous
import numpy as np
import concourse.bass as bass
import concourse.mybir as mybir
from concourse.bass_utils import run_bass_kernel_spmd

F32 = mybir.dt.float32
BF16 = mybir.dt.bfloat16
AF = mybir.ActivationFunctionType
ALU = mybir.AluOpType
f32 = np.float32

NCORES = 8
D = 1024
DC = 8
T = 8
I = 256
NR = T * I
H = 16
HD = 64
PAIRS = 8
JT = 16
EXP_SHIFT = -72.0

KB = 1024

OFF_XN = 0
OFF_WREC = 32 * KB
OFF_WF, OFF_WI, OFF_WC = OFF_WREC, OFF_WREC + 16 * KB, OFF_WREC + 32 * KB
OFF_UF, OFF_UI, OFF_UC = OFF_WREC + 48 * KB, OFF_WREC + 64 * KB, OFF_WREC + 80 * KB
OFF_WO, OFF_UO, OFF_WP = OFF_WREC, OFF_WREC + 16 * KB, OFF_WREC + 32 * KB
OFF_CPART = OFF_WREC + 48 * KB
OFF_OSTREAM = OFF_WREC + 56 * KB
OFF_EG = OFF_WREC + 72 * KB
OFF_C32 = 128 * KB
OFF_C16 = 136 * KB
OFF_FG = 140 * KB
OFF_IG = 148 * KB
OFF_GR = 156 * KB
OFF_ATTN = 160 * KB
OFF_BIAS = 200 * KB

A_Q = OFF_ATTN
A_K = OFF_ATTN + 8 * KB
A_V = OFF_ATTN + 16 * KB
A_E = OFF_ATTN + 25 * KB
A_ZR = OFF_ATTN + 29 * KB
A_ZB = OFF_ATTN + 31 * KB
A_ON = OFF_ATTN + 32 * KB
A_ONES = OFF_ATTN + 33 * KB
L_X = OFF_ATTN
L_XB = OFF_ATTN + 16 * KB
L_XQ = OFF_ATTN + 20 * KB
L_W = OFF_ATTN + 24 * KB
L_ROW = OFF_ATTN + 30 * KB
Q_WS = OFF_ATTN
Q_WB = OFF_ATTN + 8 * KB
Q_WV = OFF_ATTN + 12 * KB
Q_ST = OFF_ATTN + 28 * KB
RW_ST = OFF_FG


class Sem:
    def __init__(self, nc, name, dma, owner):
        self.name, self.dma, self.owner = name, dma, owner
        self.h = nc._sem_cm.enter_context(nc.semaphore(name))
        self.n = 0


class Prog:
    ENGINES = ("sync", "vector", "scalar", "tensor", "gpsimd")

    def __init__(self, nc):
        self.nc = nc
        self.ops = {e: [] for e in self.ENGINES}

    def emit(self, engine, fn, waits=(), inc=None, inc_k=1):
        ws = [w for w in waits if w is not None]
        post = None
        if inc is not None:
            inc.n += inc_k
            post = (inc, inc.n)
        self.ops[engine].append((fn, ws, None if inc is None else (inc, inc_k)))
        return post

    def lower(self, block):
        def mk(engine):
            def run(eng):
                hi = {}
                for fn, waits, inc in self.ops[engine]:
                    for (s, v) in waits:
                        if not s.dma and s.owner == engine:
                            continue
                        if hi.get(s.name, -1) >= v:
                            continue
                        eng.wait_ge(s.h, v)
                        hi[s.name] = v
                    instr = fn(eng)
                    if inc is not None:
                        s, k = inc
                        instr.then_inc(s.h, k)
            return run
        block.sync(mk("sync"))
        block.vector(mk("vector"))
        block.scalar(mk("scalar"))
        block.tensor(mk("tensor"))
        block.gpsimd(mk("gpsimd"))


def build_program():
    nc = bass.Bass("TRN2", target_bir_lowering=False,
                   detect_race_conditions=False)
    import contextlib
    nc._sem_cm = contextlib.ExitStack()

    xT = nc.dram_tensor("xT", [D, NR], F32, kind="ExternalInput").ap()
    wqkvT = nc.dram_tensor("wqkvT", [D, 3 * D], F32, kind="ExternalInput").ap()
    wT = {}
    for nm in ("wf", "wi", "wc", "uf", "ui", "uc", "wo", "uo", "wp"):
        wT[nm] = nc.dram_tensor(nm + "T", [D, D], F32, kind="ExternalInput").ap()
    bias_in = {}
    for nm in ("bf", "bi", "bc", "bo", "bp", "inc"):
        bias_in[nm] = nc.dram_tensor(nm + "_t", [128, 8], F32, kind="ExternalInput").ap()
    out_d = nc.dram_tensor("out", [D, NR], F32, kind="ExternalOutput").ap()

    cc_in_k = nc.dram_tensor("cc_in_k", [D, NR], BF16, kind="Internal")
    cc_in_v = nc.dram_tensor("cc_in_v", [NR, D], BF16, kind="Internal")
    cc_out_k = nc.dram_tensor("cc_out_k", [NCORES * D, NR], BF16, kind="Internal")
    cc_out_v = nc.dram_tensor("cc_out_v", [NCORES * NR, D], BF16, kind="Internal")
    bar_in = nc.dram_tensor("bar_in", [128, 1], F32, kind="Internal")
    bar_out = nc.dram_tensor("bar_out", [128, 1], F32, kind="Internal")
    q_sc = nc.dram_tensor("q_sc", [D, NR], BF16, kind="Internal")
    o_sc = nc.dram_tensor("o_sc", [D, NR], BF16, kind="Internal")

    SB_BASE = 16512
    def sb(name, shape, dtype, off):
        return nc.alloc_sbuf_tensor_at(name, shape, dtype, offset=SB_BASE + off)

    xn = sb("xn", [128, DC * NR], BF16, OFF_XN)
    w_rec = {
        "wf": sb("wf_b", [128, 8192], BF16, OFF_WF),
        "wi": sb("wi_b", [128, 8192], BF16, OFF_WI),
        "wc": sb("wc_b", [128, 8192], BF16, OFF_WC),
        "uf": sb("uf_b", [128, 8192], BF16, OFF_UF),
        "ui": sb("ui_b", [128, 8192], BF16, OFF_UI),
        "uc": sb("uc_b", [128, 8192], BF16, OFF_UC),
    }
    wo_b = sb("wo_b", [128, 8192], BF16, OFF_WO)
    uo_b = sb("uo_b", [128, 8192], BF16, OFF_UO)
    wp_b = sb("wp_b", [128, 8192], BF16, OFF_WP)
    cpart = sb("cpart", [128, 2048], F32, OFF_CPART)
    ostream = [sb(f"ostr{i}", [128, 4096], BF16, OFF_OSTREAM + i * 8 * KB)
               for i in range(2)]
    eg = [sb(f"eg{i}", [128, 512], F32, OFF_EG + i * 2 * KB) for i in range(7)]
    egx = [eg[5], sb("egx1", [128, 512], F32, OFF_EG + 14 * KB)]
    c32 = sb("c32", [128, 2048], F32, OFF_C32)
    c16 = sb("c16", [128, 2048], BF16, OFF_C16)
    tfull = sb("tf_raw", [128, 2048], F32, OFF_FG)
    tifull = sb("ti_raw", [128, 2048], F32, OFF_IG)
    grec = [sb(f"grec{i}", [128, 256], F32, OFF_GR + i * KB) for i in range(4)]
    q_ring = [sb(f"qr{i}", [128, 2048], BF16, A_Q + i * 4 * KB) for i in range(2)]
    k_ring = [sb(f"kr{i}", [128, 2048], BF16, A_K + i * 4 * KB) for i in range(2)]
    v_ring = [sb(f"vr{i}", [128, 2080], BF16, A_V + i * 4160) for i in range(2)]
    e_sl = sb("e_sl", [128, 2048], BF16, A_E)
    zrow2 = sb("zrow2", [65, 256], F32, A_ZR)
    zcomp = sb("zcomp", [128, 256], F32, OFF_ATTN + 39 * KB)
    rzb = [sb(f"rzb{i}", [128, 256], F32, OFF_EG + 16 * KB + i * KB)
           for i in range(4)]
    zsrc = [sb(f"zsrc{i}", [65, 256], F32, OFF_EG + 20 * KB + i * KB)
            for i in range(4)]
    ones65f = sb("ones65f", [128, 65], F32, OFF_ATTN + 36 * KB)
    onorm = [sb(f"onorm{i}", [128, 256], BF16, A_ON + i * 512) for i in range(2)]
    ones65 = sb("ones65", [1, 65], F32, A_ONES)
    onesD = sb("onesD", [128, 1], BF16, A_ONES + 512)
    ones1x = sb("ones1x", [1, 128], F32, A_ONES + 576)
    lx = [sb(f"lx{i}", [128, 2048], F32, L_X + i * 8 * KB) for i in range(2)]
    lxb = sb("lxb", [128, 2048], BF16, L_XB)
    lxq = sb("lxq", [128, 2048], BF16, L_XQ)
    lw = [sb(f"lw{i}", [128, 256], F32, L_W + i * KB) for i in range(6)]
    lrow = [sb(f"lrow{i}", [1, 256], F32, L_ROW + i * KB) for i in range(2)]
    qws = [sb(f"qws{i}", [128, 1024], F32, Q_WS + i * 4 * KB) for i in range(2)]
    qwb = [sb(f"qwb{i}", [128, 1024], BF16, Q_WB + i * 2 * KB) for i in range(2)]
    qwv = sb("qwv", [128, 8192], BF16, Q_WV)
    qst = ([sb(f"qst{i}", [128, 512], BF16, Q_ST + i * KB) for i in range(4)]
           + [sb(f"qst{i+4}", [128, 512], BF16, OFF_ATTN + (37 + i) * KB)
              for i in range(2)])
    rws = [sb(f"rws{i}", [128, 2048], F32, RW_ST + i * 8 * KB) for i in range(2)]
    bias_t = {nm: sb(nm + "_sb", [128, 8], F32, OFF_BIAS + j * 32)
              for j, nm in enumerate(("bf", "bi", "bc", "bo", "bp", "inc",
                                      "bfh", "bih"))}

    pbA = nc.alloc_psum_tensor("pbA", [128, 1024], F32)
    pbB = nc.alloc_psum_tensor("pbB", [128, 1024], F32)
    pb = [None, None, None, None] + [
        nc.alloc_psum_tensor(f"pb{i}", [128, 512], F32) for i in range(4, 8)]

    P = Prog(nc)
    s_x = Sem(nc, "s_x", True, "sync")
    s_wr = Sem(nc, "s_wr", True, "sync")
    s_st = Sem(nc, "s_st", True, "sync")
    s_eg = Sem(nc, "s_eg", True, "sync")
    s_out = Sem(nc, "s_out", True, "sync")
    s_g = Sem(nc, "s_g", True, "gpsimd")
    s_cb = Sem(nc, "s_cb", True, "gpsimd")
    s_ck = Sem(nc, "s_ck", True, "gpsimd")
    s_cv = Sem(nc, "s_cv", True, "gpsimd")
    s_bias = Sem(nc, "s_bias", True, "sync")
    sm_lx = [Sem(nc, f"s_lx{i}", True, "sync") for i in range(2)]
    sm_ws = [Sem(nc, f"s_ws{i}", True, "sync") for i in range(8)]
    sm_rw = [Sem(nc, f"s_rw{i}", True, "sync") for i in range(4)]
    sm_st = [Sem(nc, f"s_stg{i}", True, "gpsimd") for i in range(6)]
    sm_st2 = [Sem(nc, f"s_sth{i}", True, "sync") for i in range(6)]
    sm_kr = [Sem(nc, f"s_kr{i}", True, "sync") for i in range(2)]
    sm_vr = [Sem(nc, f"s_vr{i}", True, "sync") for i in range(2)]
    sm_qr = [Sem(nc, f"s_qr{i}", True, "sync") for i in range(2)]
    sm_on = [Sem(nc, f"s_on{i}", True, "gpsimd") for i in range(2)]
    sm_zc = [Sem(nc, f"s_zc{i}", True, "gpsimd") for i in range(2)]
    sm_zs = [Sem(nc, f"s_zs{i}", True, "gpsimd") for i in range(4)]
    sm_egw = [Sem(nc, f"s_egw{i}", True, "sync") for i in range(2)]
    sm_os = [Sem(nc, f"s_os{i}", True, "sync") for i in range(2)]
    s_xe = Sem(nc, "s_xe", True, "sync")
    s_pe = Sem(nc, "s_pe", False, "tensor")
    s_ac = Sem(nc, "s_ac", False, "scalar")
    s_dv = Sem(nc, "s_dv", False, "vector")
    s_pl = Sem(nc, "s_pl", False, "gpsimd")

    E = P.emit

    ev_bias = None
    for nm in ("bf", "bi", "bc", "bo", "bp", "inc"):
        ev_bias = E("sync", lambda eng, nm=nm: eng.dma_start(
            out=bias_t[nm][:, :], in_=bias_in[nm]), inc=s_bias, inc_k=16)
    ev_onesD = E("vector", lambda eng: eng.memset(onesD[:, :], 1.0), inc=s_dv)
    E("vector", lambda eng: eng.memset(ones65f[:, :], 1.0))
    ev_ones1x = E("vector", lambda eng: eng.memset(ones1x[:, :], 1.0), inc=s_dv)
    ev_ones = None
    ev_bh = [None, None]
    ev_bh[0] = E("vector", lambda eng: eng.tensor_scalar_mul(
        bias_t["bfh"][:, :], bias_t["bf"][:, :], 0.5), waits=[ev_bias], inc=s_dv)
    ev_bh[1] = E("vector", lambda eng: eng.tensor_scalar_mul(
        bias_t["bih"][:, :], bias_t["bi"][:, :], 0.5), waits=[ev_bias], inc=s_dv)
    ev_cz = E("vector", lambda eng: eng.memset(c32[:, :], 0.0), inc=s_dv)
    ev_c0 = None
    for kc in range(DC):
        ev_c0 = E("scalar", lambda eng, kc=kc: eng.activation(
            c32[:, kc * 256:(kc + 1) * 256], c32[:, kc * 256:(kc + 1) * 256],
            AF.Identity, scale=f32(0.0), bias=bias_t["inc"][:, kc:kc + 1]),
            waits=[ev_bias, ev_cz], inc=s_ac)
    ev_c16_0 = E("vector", lambda eng: eng.tensor_copy(c16[:, :], c32[:, :]),
                 waits=[ev_c0], inc=s_dv)

    rw_names = ("wf", "uf", "wi", "ui", "wc", "uc")
    rw_cast_evt = {}
    rw_cast_by_slot = {}
    rw_slot = [0]

    def emit_rw_slice():
        slot = rw_slot[0]
        if slot >= 24:
            return
        rw_slot[0] += 1
        nm = rw_names[slot // 4]
        cs = slot % 4
        cur = slot % 2
        waits = []
        if slot >= 2:
            waits.append(rw_cast_by_slot[slot - 2])
        ev_d = E("sync", lambda eng, nm=nm, cs=cs, cur=cur: eng.dma_start(
            out=rws[cur][:, :],
            in_=wT[nm][:, cs * 256:(cs + 1) * 256].rearrange(
                "(kc p) n -> p kc n", kc=DC)),
            waits=waits, inc=sm_rw[cur], inc_k=16)
        ev_c = E("scalar", lambda eng, nm=nm, cs=cs, cur=cur: eng.copy(
            w_rec[nm][:, :].rearrange("p (kc m) -> p kc m", kc=DC)[
                :, :, cs * 256:(cs + 1) * 256],
            rws[cur][:, :].rearrange("p (kc n) -> p kc n", kc=DC)),
            waits=[ev_d], inc=s_ac)
        rw_cast_by_slot[slot] = ev_c
        if cs == 3:
            rw_cast_evt[nm] = ev_c

    rw_all_box = [None]

    lx_ev = [None] * T
    prev = {}
    for s in range(min(2, T)):
        lx_ev[s] = E("sync", lambda eng, s=s: eng.dma_start(
            out=lx[s % 2][:, :],
            in_=xT[:, s * 256:(s + 1) * 256].rearrange("(c p) n -> p c n", c=DC)),
            inc=sm_lx[s % 2], inc_k=16)

    ev_ln_done = None
    pending_apply = [None]
    for s in range(T):
        xs = lx[s % 2]
        ev_cb2 = E("vector", lambda eng, xs=xs: eng.tensor_copy(lxb[:, :], xs[:, :]),
                   waits=[lx_ev[s], prev.get("s1")], inc=s_dv)
        ev_sq = E("vector", lambda eng, xs=xs: eng.tensor_tensor(
            lxq[:, :], xs[:, :], xs[:, :], ALU.mult),
            waits=[prev.get("s2")], inc=s_dv)
        if pending_apply[0] is not None:
            pending_apply[0]()
        ev_s1 = ev_s2 = None
        for c in range(DC):
            ev_s1 = E("tensor", lambda eng, c=c: eng.matmul(
                pbA[0:1, 0:256], onesD[:, :], lxb[:, c * 256:(c + 1) * 256],
                start=(c == 0), stop=(c == DC - 1)),
                waits=[ev_cb2, ev_onesD, prev.get("r1")] if c == 0 else [], inc=s_pe)
        for c in range(DC):
            ev_s2 = E("tensor", lambda eng, c=c: eng.matmul(
                pbA[0:1, 512:768], onesD[:, :], lxq[:, c * 256:(c + 1) * 256],
                start=(c == 0), stop=(c == DC - 1)),
                waits=[ev_sq, prev.get("r2")] if c == 0 else [], inc=s_pe)
        ev_r1 = E("scalar", lambda eng: eng.copy(lrow[0][:, :], pbA[0:1, 0:256]),
                  waits=[ev_s1, prev.get("b1")], inc=s_ac)
        ev_r2 = E("scalar", lambda eng: eng.copy(lrow[1][:, :], pbA[0:1, 512:768]),
                  waits=[ev_s2, prev.get("b2")], inc=s_ac)
        ev_b1 = E("tensor", lambda eng: eng.matmul(
            pbB[:, 0:256], ones1x[:, :], lrow[0][:, :], start=True, stop=True),
            waits=[ev_r1, prev.get("m")], inc=s_pe)
        ev_b2 = E("tensor", lambda eng: eng.matmul(
            pbB[:, 512:768], ones1x[:, :], lrow[1][:, :], start=True, stop=True),
            waits=[ev_r2, prev.get("mq")], inc=s_pe)
        ev_m = E("vector", lambda eng: eng.tensor_scalar_mul(
            lw[0][:, :], pbB[:, 0:256], 1.0 / D), waits=[ev_b1], inc=s_dv)
        ev_mq = E("vector", lambda eng: eng.tensor_scalar_mul(
            lw[1][:, :], pbB[:, 512:768], 1.0 / D), waits=[ev_b2], inc=s_dv)
        E("vector", lambda eng: eng.tensor_tensor(
            lw[2][:, :], lw[0][:, :], lw[0][:, :], ALU.mult), inc=s_dv)
        ev_var = E("vector", lambda eng: eng.tensor_tensor(
            lw[3][:, :], lw[1][:, :], lw[2][:, :], ALU.subtract),
            waits=[prev.get("ln")], inc=s_dv)
        ev_ln = E("scalar", lambda eng: eng.activation(
            lw[4][:, :], lw[3][:, :], AF.Ln, bias=f32(1e-5)),
            waits=[ev_var], inc=s_ac)
        ev_al = E("scalar", lambda eng: eng.activation(
            lw[5][:, :], lw[4][:, :], AF.Exp, scale=f32(-0.5)), inc=s_ac)
        def mk_apply(s=s, xs=xs, ev_al=ev_al, ev_m=ev_m):
            def go():
                ev_ap = None
                for c in range(DC):
                    E("vector", lambda eng, xs=xs, c=c: eng.tensor_tensor(
                        lw[2][:, :], xs[:, c * 256:(c + 1) * 256], lw[0][:, :],
                        ALU.subtract), waits=[ev_al, ev_m], inc=s_dv)
                    ev_ap = E("vector", lambda eng, s=s, c=c: eng.tensor_tensor(
                        xn[:, c * 2048 + s * 256: c * 2048 + (s + 1) * 256],
                        lw[2][:, :], lw[5][:, :], ALU.mult), inc=s_dv)
                if s + 2 < T:
                    lx_ev[s + 2] = E("sync", lambda eng, s=s: eng.dma_start(
                        out=lx[s % 2][:, :],
                        in_=xT[:, (s + 2) * 256:(s + 3) * 256].rearrange(
                            "(c p) n -> p c n", c=DC)),
                        waits=[ev_ap], inc=sm_lx[s % 2], inc_k=16)
                return ev_ap
            return go
        pending_apply[0] = mk_apply()
        prev = {"s1": ev_s1, "s2": ev_s2, "r1": ev_r1, "r2": ev_r2,
                "b1": ev_b1, "b2": ev_b2, "m": ev_m, "mq": ev_mq, "ln": ev_ln}
        emit_rw_slice()
        emit_rw_slice()
        emit_rw_slice()
    ev_ln_done = pending_apply[0]()


    rec_units = []
    rec_last_cmm = {}
    rec_cast_evt = {-1: ev_c16_0}
    rec_state = {"bank": 0, "evac": {}, "c_upd": {}}

    def mk_rec_unit(t, gi, mc):
        gamma = ("f", "i", "g")[gi]
        wname = {"f": "wf", "i": "wi", "g": "wc"}[gamma]
        uname = {"f": "uf", "i": "ui", "g": "uc"}[gamma]

        def unit():
            k = rec_state["bank"]
            bank = pb[6 + k % 2]
            rec_state["bank"] = k + 1
            waits0 = [rw_all_box[0], ev_ln_done,
                      rec_cast_evt.get(t - 1), rec_state["evac"].get(k - 2)]
            mmev = None
            for kc in range(DC):
                mmev = E("tensor", lambda eng, kc=kc, bank=bank: eng.matmul(
                    bank[:, 0:256],
                    w_rec[wname][:, kc * 1024 + mc * 128: kc * 1024 + (mc + 1) * 128],
                    xn[:, kc * 2048 + t * 256: kc * 2048 + (t + 1) * 256],
                    start=(kc == 0), stop=False),
                    waits=waits0 if kc == 0 else [], inc=s_pe)
            for kc in range(DC):
                mmev = E("tensor", lambda eng, kc=kc, bank=bank: eng.matmul(
                    bank[:, 0:256],
                    w_rec[uname][:, kc * 1024 + mc * 128: kc * 1024 + (mc + 1) * 128],
                    c16[:, kc * 256:(kc + 1) * 256],
                    start=False, stop=(kc == DC - 1)), inc=s_pe)
            if gamma == "g":
                rec_last_cmm[t] = mmev
            if gamma == "f":
                ev_e = E("scalar", lambda eng, bank=bank: eng.activation(
                    tfull[:, mc * 256:(mc + 1) * 256], bank[:, 0:256], AF.Tanh,
                    scale=f32(0.5), bias=bias_t["bfh"][:, mc:mc + 1]),
                    waits=[mmev, ev_bh[0]], inc=s_ac)
            elif gamma == "i":
                ev_e = E("scalar", lambda eng, bank=bank: eng.activation(
                    tifull[:, mc * 256:(mc + 1) * 256], bank[:, 0:256], AF.Tanh,
                    scale=f32(0.5), bias=bias_t["bih"][:, mc:mc + 1]),
                    waits=[mmev, ev_bh[1]], inc=s_ac)
            else:
                gslot = grec[mc % 2]
                ev_e = E("scalar", lambda eng, bank=bank, gslot=gslot: eng.activation(
                    gslot[:, :], bank[:, 0:256], AF.Tanh,
                    scale=f32(1.0), bias=bias_t["bc"][:, mc:mc + 1]),
                    waits=[mmev, rec_state["c_upd"].get((t, mc - 2))], inc=s_ac)
                t1, t2 = grec[2], grec[3]
                E("vector", lambda eng: eng.scalar_tensor_tensor(
                    t1[:, :], tfull[:, mc * 256:(mc + 1) * 256], 1.0,
                    c32[:, mc * 256:(mc + 1) * 256], ALU.add, ALU.mult),
                    waits=[ev_e], inc=s_dv)
                ev_t2 = E("vector", lambda eng, gslot=gslot: eng.scalar_tensor_tensor(
                    t2[:, :], tifull[:, mc * 256:(mc + 1) * 256], 1.0,
                    gslot[:, :], ALU.add, ALU.mult), inc=s_dv)
                rec_state["c_upd"][(t, mc)] = ev_t2
                E("vector", lambda eng: eng.tensor_tensor(
                    t1[:, :], t1[:, :], t2[:, :], ALU.add), inc=s_dv)
                E("vector", lambda eng: eng.tensor_scalar_mul(
                    c32[:, mc * 256:(mc + 1) * 256], t1[:, :], 0.5), inc=s_dv)
                if mc == DC - 1:
                    rec_cast_evt[t] = E("vector", lambda eng: eng.tensor_copy(
                        c16[:, :], c32[:, :]),
                        waits=[rec_last_cmm[t]], inc=s_dv)
            rec_state["evac"][k] = ev_e
        return unit

    for t in range(T):
        for gi in range(3):
            for mc in range(DC):
                rec_units.append(mk_rec_unit(t, gi, mc))
    ri = [0]

    def pump_rec(nun):
        for _ in range(nun):
            if ri[0] < len(rec_units):
                rec_units[ri[0]]()
                ri[0] += 1

    rw_all_box[0] = rw_cast_by_slot[23]

    ms_cast = {}
    ms_dma = {}
    ms_mm_last = {}
    st_free = {}
    evac_evts = []
    st_i = [0]
    kv_dma_last = [None]
    qsc_dma_last = [None]
    qwb_prev = {}
    qkv_group = [0]

    ms_order = list(range(8, 24)) + list(range(0, 8))
    seq_of = {ms: i for i, ms in enumerate(ms_order)}

    def wslice_load(seq):
        ms = ms_order[seq]
        widx = seq % 2
        if seq >= 2:
            waits = [ms_cast[ms_order[seq - 2]]]
        else:
            waits = [ev_ln_done]
        ms_dma[ms] = E("sync", lambda eng, ms=ms, widx=widx: eng.dma_start(
            out=qws[widx][:, :],
            in_=wqkvT[:, ms * 128:(ms + 1) * 128].rearrange(
                "(kc p) n -> p kc n", kc=DC)),
            waits=waits, inc=sm_ws[widx], inc_k=16)

    def wslice_cast(seq):
        ms = ms_order[seq]
        widx = seq % 2
        waits = [ms_dma[ms]]
        if ms < 16:
            waits.append(ms_mm_last.get(qwb_prev.get(seq % 2)))
            qwb_prev[seq % 2] = ms
            if seq >= 16:
                ev = E("scalar", lambda eng, widx=widx, seq=seq: eng.copy(
                    qwb[seq % 2][:, :], qws[widx][:, :]), waits=waits, inc=s_ac)
            else:
                ev = E("vector", lambda eng, widx=widx, seq=seq: eng.tensor_copy(
                    qwb[seq % 2][:, :], qws[widx][:, :]), waits=waits, inc=s_dv)
        else:
            ev = E("vector", lambda eng, ms=ms, widx=widx: eng.tensor_copy(
                qwv[:, :].rearrange("p (kc m) -> p kc m", kc=DC)[
                    :, :, (ms - 16) * 128:(ms - 15) * 128],
                qws[widx][:, :].rearrange("p (kc n) -> p kc n", kc=DC)),
                waits=waits, inc=s_dv)
        ms_cast[ms] = ev

    def qkv_qk_group(ms, nck):
        bank = pb[4 + (qkv_group[0] % 2)]
        qkv_group[0] += 1
        mmev = None
        for kc in range(DC):
            w0 = []
            if kc == 0:
                w0 = [ms_cast[ms]]
                if len(evac_evts) >= 2:
                    w0.append(evac_evts[-2])
            mmev = E("tensor", lambda eng, ms=ms, nck=nck, kc=kc, bank=bank:
                     eng.matmul(
                         bank[:, :],
                         qwb[seq_of[ms] % 2][:, kc * 128:(kc + 1) * 128],
                         xn[:, kc * 2048 + nck * 512: kc * 2048 + (nck + 1) * 512],
                         start=(kc == 0), stop=(kc == DC - 1)),
                     waits=w0, inc=s_pe)
        ms_mm_last[ms] = mmev
        sl = st_i[0] % 6
        st_i[0] += 1
        ev_e = E("scalar", lambda eng, sl=sl, bank=bank: eng.copy(
            qst[sl][:, :], bank[:, :]),
            waits=[mmev, st_free.get(sl)], inc=s_ac)
        evac_evts.append(ev_e)
        if ms < 8:
            ev_w = E("sync", lambda eng, ms=ms, nck=nck, sl=sl: eng.dma_start(
                out=q_sc.ap()[ms * 128:(ms + 1) * 128, nck * 512:(nck + 1) * 512],
                in_=qst[sl][:, :]), waits=[ev_e], inc=sm_st2[sl], inc_k=16)
            qsc_dma_last[0] = ev_w
        else:
            ev_w = E("gpsimd", lambda eng, ms=ms, nck=nck, sl=sl: eng.dma_start(
                out=cc_in_k.ap()[(ms - 8) * 128:(ms - 7) * 128,
                                 nck * 512:(nck + 1) * 512],
                in_=qst[sl][:, :]), waits=[ev_e], inc=sm_st[sl], inc_k=16)
            kv_dma_last[0] = ev_w
        st_free[sl] = ev_w

    def qkv_v_group(rt, dvc):
        bank = pb[4 + (qkv_group[0] % 2)]
        qkv_group[0] += 1
        mmev = None
        for kc in range(DC):
            w0 = []
            if kc == 0:
                w0 = [ms_cast[23]]
                if len(evac_evts) >= 2:
                    w0.append(evac_evts[-2])
            mmev = E("tensor", lambda eng, rt=rt, dvc=dvc, kc=kc, bank=bank:
                     eng.matmul(
                         bank[:, :],
                         xn[:, kc * 2048 + rt * 128: kc * 2048 + (rt + 1) * 128],
                         qwv[:, kc * 1024 + dvc * 512: kc * 1024 + (dvc + 1) * 512],
                         start=(kc == 0), stop=(kc == DC - 1)),
                     waits=w0, inc=s_pe)
        sl = st_i[0] % 6
        st_i[0] += 1
        ev_e = E("scalar", lambda eng, sl=sl, bank=bank: eng.copy(
            qst[sl][:, :], bank[:, :]),
            waits=[mmev, st_free.get(sl)], inc=s_ac)
        evac_evts.append(ev_e)
        ev_w = E("sync", lambda eng, rt=rt, dvc=dvc, sl=sl: eng.dma_start(
            out=cc_in_v.ap()[rt * 128:(rt + 1) * 128, dvc * 512:(dvc + 1) * 512],
            in_=qst[sl][:, :]), waits=[ev_e], inc=sm_st2[sl], inc_k=16)
        st_free[sl] = ev_w
        kv_dma_last[0] = ev_w
        v_mm_last[0] = mmev

    v_mm_last = [None]
    wslice_load(0)
    wslice_load(1)
    for seq in range(8):
        wslice_cast(seq)
        if seq + 2 < 24:
            wslice_load(seq + 2)
        for nck in range(4):
            qkv_qk_group(ms_order[seq], nck)
    for seq in range(8, 16):
        wslice_cast(seq)
        if seq + 2 < 24:
            wslice_load(seq + 2)
    E("gpsimd", lambda eng: eng.collective_compute(
        "AllGather", ALU.bypass, replica_groups=[list(range(NCORES))],
        ins=[cc_in_k.ap().opt()], outs=[cc_out_k.ap().opt()]),
        waits=[st_free.get(i) for i in range(6)],
        inc=s_ck, inc_k=1)
    for rt in range(16):
        for dvc in range(2):
            qkv_v_group(rt, dvc)
            pump_rec(3)

    ev_ones = None
    for r in range(2):
        for col in (0, 129):
            ev_ones = E("vector", lambda eng, r=r, col=col: eng.memset(
                v_ring[r][:, :].rearrange("p (j q) -> p j q", j=JT)[
                    :, :, col:col + 1], 1.0),
                waits=[v_mm_last[0]], inc=s_dv)

    E("gpsimd", lambda eng: eng.collective_compute(
        "AllGather", ALU.bypass, replica_groups=[list(range(NCORES))],
        ins=[cc_in_v.ap().opt()], outs=[cc_out_v.ap().opt()]),
        waits=[st_free.get(i) for i in range(6)],
        inc=s_cv, inc_k=1)

    pump_rec(32)

    for seq in range(16, 24):
        wslice_cast(seq)
        if seq + 2 < 24:
            wslice_load(seq + 2)
        for nck in range(4):
            qkv_qk_group(ms_order[seq], nck)

    qkv_mm_done = [ms_mm_last[ms_order[23]]]
    qkv_cast_done = [ms_cast[ms_order[23]]]

    attn_units = []
    att_state = {"dma": {}, "pe_done": {}, "norm": {}, "q_dma": {},
                 "q_used": {}, "osc": [None], "on0": None, "on1": None,
                 "zd": {}}

    def emit_attn_dma(n):
        if n >= 64:
            return
        b, pr = divmod(n, PAIRS)
        r = n % 2
        waits = [(s_ck, 1), att_state["pe_done"].get(n - 2)]
        waits_v = [(s_cv, 1), att_state["pe_done"].get(n - 2)]
        if n < 2:
            waits += [qkv_mm_done[0], qkv_cast_done[0]]
            waits_v += [qkv_mm_done[0], qkv_cast_done[0]]
        if pr == 0:
            att_state["q_dma"][b] = E("sync", lambda eng, b=b: eng.dma_start(
                out=q_ring[b % 2][:, :].rearrange("p (c i) -> p c i", c=DC),
                in_=q_sc.ap().rearrange("(c p) (t i) -> p c t i",
                                        c=DC, t=T)[:, :, b, :]),
                waits=[st_free.get(i) for i in range(6)]
                      + [att_state["q_used"].get(b - 2), qkv_mm_done[0]],
                inc=sm_qr[b % 2 if True else 0], inc_k=16)
        ev_k = E("sync", lambda eng, b=b, pr=pr, r=r: eng.dma_start(
            out=k_ring[r][:, :].rearrange("p (rr i) -> p rr i", rr=NCORES),
            in_=cc_out_k.ap().rearrange("(rr p) (t i) -> p rr t i",
                                        rr=NCORES, t=T)[
                pr * 128:(pr + 1) * 128, :, b, :]),
            waits=waits, inc=sm_kr[r], inc_k=16)
        ev_v = None
        for ih in range(2):
            ev_v = E("sync", lambda eng, b=b, pr=pr, r=r, ih=ih: eng.dma_start(
                out=v_ring[r][:, :].rearrange("p (rr ih q) -> p rr ih q",
                                              rr=NCORES, ih=2)[:, :, ih, 1:129],
                in_=cc_out_v.ap().rearrange("(rr t ih p) c -> t ih p rr c",
                                            rr=NCORES, t=T, ih=2)[
                    b, ih][:, :, pr * 128:(pr + 1) * 128]),
                waits=waits_v, inc=sm_vr[r], inc_k=16)
        att_state["dma"][n] = (ev_k, ev_v)

    def mk_attn_unit(n):
        b, pr = divmod(n, PAIRS)
        r = n % 2

        def unit():
            ev_k, ev_v = att_state["dma"][n]
            ev_q = att_state["q_dma"][b]
            kt, vt, qt = k_ring[r], v_ring[r], q_ring[b % 2]
            exp_evts = {}
            pv_evts = {}
            mm_o = None
            prev_na = att_state.get("norm_a")
            prev_norm = att_state["norm"].get(n - 1)

            def emit_S(jt):
                sT = pbA if jt % 2 == 0 else pbB
                cb = 0
                w_s = [ev_k, ev_q, exp_evts.get(jt - 2)]
                E("tensor", lambda eng, jt=jt, sT=sT, cb=cb: eng.matmul(
                    sT[:, cb:cb + 256], kt[0:64, jt * 128:(jt + 1) * 128],
                    qt[0:64, pr * 256:(pr + 1) * 256], start=True, stop=True),
                    waits=w_s, inc=s_pe)
                ev_sb = E("tensor", lambda eng, jt=jt, sT=sT, cb=cb: eng.matmul(
                    sT[:, 512 + cb:768 + cb],
                    kt[64:128, jt * 128:(jt + 1) * 128],
                    qt[64:128, pr * 256:(pr + 1) * 256], start=True, stop=True),
                    inc=s_pe)
                esl = jt % 4
                ev_x = E("scalar", lambda eng, sT=sT, esl=esl, cb=cb:
                         eng.activation(
                             e_sl[:, esl * 512:(esl + 1) * 512].rearrange(
                                 "p (g c) -> p g c", g=2),
                             sT[:, :].rearrange("p (g c) -> p g c", g=2)[
                                 :, :, cb:cb + 256],
                             AF.Exp, scale=f32(0.125), bias=f32(EXP_SHIFT)),
                         waits=[ev_sb, pv_evts.get(jt - 4)], inc=s_ac)
                exp_evts[jt] = ev_x

            def emit_PV(jt):
                nonlocal mm_o
                esl = jt % 4
                w_pv = [exp_evts[jt], ev_v, ev_ones]
                w_pvb = []
                if jt == 0:
                    w_pv.append(prev_na)
                    w_pvb = [prev_norm]
                E("tensor", lambda eng, jt=jt, esl=esl: eng.matmul(
                    pb[4][0:65, 0:256], vt[:, jt * 130: jt * 130 + 65],
                    e_sl[:, esl * 512: esl * 512 + 256],
                    start=(jt == 0), stop=(jt == JT - 1)), waits=w_pv, inc=s_pe)
                mm_o = E("tensor", lambda eng, jt=jt, esl=esl: eng.matmul(
                    pb[5][0:65, 0:256], vt[:, jt * 130 + 65: jt * 130 + 130],
                    e_sl[:, esl * 512 + 256: esl * 512 + 512],
                    start=(jt == 0), stop=(jt == JT - 1)), waits=w_pvb, inc=s_pe)
                pv_evts[jt] = mm_o

            emit_S(0)
            emit_S(1)
            for jt in range(2, JT):
                emit_S(jt)
                emit_PV(jt - 2)
            emit_PV(JT - 2)
            emit_PV(JT - 1)
            att_state["pe_done"][n] = mm_o
            if pr == PAIRS - 1:
                att_state["q_used"][b] = mm_o
            zda, zdb = att_state["zd"].get(n - 1, (None, None))
            ev_na = E("vector", lambda eng: eng.tensor_copy(
                onorm[0][0:65, :], pb[4][0:65, 0:256]),
                waits=[mm_o, att_state["on0"]], inc=s_dv)
            ev_za = E("vector", lambda eng: eng.tensor_copy(
                zrow2[0:1, :], pb[4][0:1, 0:256]), waits=[zda], inc=s_dv)
            ev_nb = E("vector", lambda eng: eng.tensor_copy(
                onorm[1][0:64, :], pb[5][0:64, 0:256]),
                waits=[att_state["on1"]], inc=s_dv)
            ev_zb = E("vector", lambda eng: eng.tensor_copy(
                zrow2[64:65, :], pb[5][64:65, 0:256]), waits=[zdb], inc=s_dv)
            att_state["norm_a"] = ev_za
            att_state["norm"][n] = ev_zb
            ev_wa = E("gpsimd", lambda eng, b=b, pr=pr: eng.dma_start(
                out=o_sc.ap()[pr * 128: pr * 128 + 64, b * 256:(b + 1) * 256],
                in_=onorm[0][1:65, :]), waits=[ev_na], inc=sm_on[0], inc_k=16)
            ev_wb = E("gpsimd", lambda eng, b=b, pr=pr: eng.dma_start(
                out=o_sc.ap()[pr * 128 + 64: pr * 128 + 128, b * 256:(b + 1) * 256],
                in_=onorm[1][0:64, :]), waits=[ev_nb], inc=sm_on[1], inc_k=16)
            ev_da = E("gpsimd", lambda eng, n=n: eng.dma_start(
                out=zcomp[2 * n:2 * n + 1, :],
                in_=zrow2[0:1, :]), waits=[ev_za], inc=sm_zc[0], inc_k=16)
            ev_db = E("gpsimd", lambda eng, n=n: eng.dma_start(
                out=zcomp[2 * n + 1:2 * n + 2, :],
                in_=zrow2[64:65, :]), waits=[ev_zb], inc=sm_zc[1], inc_k=16)
            att_state["zd"][n] = (ev_da, ev_db)
            att_state["on0"] = ev_wa
            att_state["on1"] = ev_wb
            att_state["osc"][0] = ev_wb
            emit_attn_dma(n + 2)
        return unit

    for n in range(64):
        attn_units.append(mk_attn_unit(n))

    eg_cast = {}
    eg_stage = {}
    egslot = [0]

    eg_names = [("wo", wo_b), ("uo", uo_b), ("wp", wp_b)]
    eg_dma = {}

    def emit_eg_stage_all():
        for sl in range(12):
            nm, dst = eg_names[sl // 4]
            cs = sl % 4
            cur = sl % 2
            waits = [rec_cast_evt[T - 1], rec_last_cmm[T - 1],
                     eg_stage.get(sl - 2)]
            ev_d = E("sync", lambda eng, nm=nm, cs=cs, cur=cur:
                     eng.dma_start(
                         out=rws[cur][:, :],
                         in_=wT[nm][:, cs * 256:(cs + 1) * 256].rearrange(
                             "(kc p) n -> p kc n", kc=DC)),
                     waits=waits, inc=sm_egw[cur], inc_k=16)
            ev_c = E("vector", lambda eng, dst=dst, cs=cs, cur=cur:
                     eng.tensor_copy(
                         dst[:, :].rearrange("p (kc m) -> p kc m", kc=DC)[
                             :, :, cs * 256:(cs + 1) * 256],
                         rws[cur][:, :].rearrange("p (kc n) -> p kc n",
                                                  kc=DC)),
                     waits=[ev_d], inc=s_dv)
            eg_stage[sl] = ev_c
            if cs == 3:
                eg_cast[nm] = ev_c

    emit_attn_dma(0)
    emit_attn_dma(1)
    for n in range(64):
        attn_units[n]()
        pump_rec(1)
    pump_rec(len(rec_units))
    emit_eg_stage_all()
    ev_rec_done = rec_cast_evt[T - 1]
    ev_rz = E("vector", lambda eng: eng.reciprocal(zcomp[:, :], zcomp[:, :]),
              waits=[(sm_zc[0], sm_zc[0].n), (sm_zc[1], sm_zc[1].n)], inc=s_dv)
    rz_state = {"pbA": None, "pbB": None,
                "used": {}, "ev": {}, "src_used": {}}

    def emit_rz_half(ti, hb, qc, kc, ring, ev_o):
        j = (2 * ti + hb) % 4
        b = 2 * qc + hb
        rA = 2 * (b * 8 + kc)
        rB = rA + 1
        dA = E("gpsimd", lambda eng, j=j, rA=rA: eng.dma_start(
            out=zsrc[j][0:1, :], in_=zcomp[rA:rA + 1, :]),
            waits=[ev_rz, rz_state["src_used"].get(j)],
            inc=sm_zs[j], inc_k=16)
        dB = E("gpsimd", lambda eng, j=j, rB=rB: eng.dma_start(
            out=zsrc[j][64:65, :], in_=zcomp[rB:rB + 1, :]),
            inc=sm_zs[j], inc_k=16)
        bcA = E("tensor", lambda eng, j=j: eng.matmul(
            pb[4][0:64, 0:256], ones65f[0:1, 0:64], zsrc[j][0:1, :],
            start=True, stop=True),
            waits=[dA, dB, rz_state["pbA"], rz_state["used"].get(j)],
            inc=s_pe)
        bcB = E("tensor", lambda eng, j=j: eng.matmul(
            pb[5][64:128, 0:256], ones65f[64:65, 0:64], zsrc[j][64:65, :],
            start=True, stop=True), waits=[rz_state["pbB"]], inc=s_pe)
        evA = E("vector", lambda eng, j=j: eng.tensor_copy(
            rzb[j][0:64, :], pb[4][0:64, 0:256]), waits=[bcB], inc=s_dv)
        evB = E("vector", lambda eng, j=j: eng.tensor_copy(
            rzb[j][64:128, :], pb[5][64:128, 0:256]), inc=s_dv)
        rz_state["pbA"] = evA
        rz_state["pbB"] = evB
        rz_state["src_used"][j] = bcB
        ev_nz = E("vector", lambda eng, j=j, kc=kc, hb=hb, ring=ring:
                  eng.tensor_tensor(
                      ring[:, kc * 512 + hb * 256: kc * 512 + (hb + 1) * 256],
                      ring[:, kc * 512 + hb * 256: kc * 512 + (hb + 1) * 256],
                      rzb[j][:, :], ALU.mult),
                  waits=[evB, ev_o], inc=s_dv)
        rz_state["used"][j] = ev_nz
        return ev_nz
    ev_attn_done = E("gpsimd", lambda eng: eng.nop(),
                     waits=[att_state["on0"], att_state["on1"]], inc=s_pl)

    ev_cp_evac = {}
    ev_cpart = None
    for mc in range(DC):
        bank = pb[6 + mc % 2]
        mmev = None
        for kc in range(DC):
            w0 = [eg_cast["uo"], ev_rec_done, ev_cp_evac.get(mc - 2)] if kc == 0 else []
            mmev = E("tensor", lambda eng, mc=mc, kc=kc, bank=bank: eng.matmul(
                bank[:, 0:256],
                uo_b[:, kc * 1024 + mc * 128: kc * 1024 + (mc + 1) * 128],
                c16[:, kc * 256:(kc + 1) * 256],
                start=(kc == 0), stop=(kc == DC - 1)), waits=w0, inc=s_pe)
        ev_cpart = E("scalar", lambda eng, mc=mc, bank=bank: eng.copy(
            cpart[:, mc * 256:(mc + 1) * 256], bank[:, 0:256]),
            waits=[mmev], inc=s_ac)
        ev_cp_evac[mc] = ev_cpart

    ost_ev = {}
    ring_rd = {}
    egp = {}
    ev_out_last = None
    for qc in range(4):
        ring = ostream[qc % 2]
        ev_o = E("sync", lambda eng, qc=qc, ring=ring: eng.dma_start(
            out=ring[:, :].rearrange("p (c n) -> p c n", c=DC),
            in_=o_sc.ap()[:, qc * 512:(qc + 1) * 512].rearrange(
                "(c p) n -> p c n", c=DC)),
            waits=[ev_attn_done, rec_last_cmm[T - 1], ring_rd.get(qc - 2)],
            inc=sm_os[qc % 2], inc_k=16)
        ev_nz_last = None
        for kc in range(DC):
            for hb in range(2):
                ev_nz_last = emit_rz_half(qc * 8 + kc, hb, qc, kc, ring, ev_o)
        for mc in range(DC):
            par = mc % 2
            bank_w = pbA[:, par * 512:(par + 1) * 512]
            bank_p = pbB[:, par * 512:(par + 1) * 512]
            xi = (qc * 8 + mc) % 2
            ev_x = E("sync", lambda eng, qc=qc, mc=mc, xi=xi: eng.dma_start(
                out=egx[xi][:, :],
                in_=xT[mc * 128:(mc + 1) * 128, qc * 512:(qc + 1) * 512]),
                waits=[egp.get(("fin", xi), ev_attn_done)], inc=s_xe, inc_k=16)
            mm_w = None
            for kc in range(DC):
                w0 = [eg_cast["wo"], ev_attn_done,
                      egp.get(("t1", par))] if kc == 0 else []
                mm_w = E("tensor", lambda eng, qc=qc, mc=mc, kc=kc, bank_w=bank_w:
                         eng.matmul(
                             bank_w[:, :],
                             wo_b[:, kc * 1024 + mc * 128: kc * 1024 + (mc + 1) * 128],
                             xn[:, kc * 2048 + qc * 512: kc * 2048 + (qc + 1) * 512],
                             start=(kc == 0), stop=(kc == DC - 1)),
                         waits=w0, inc=s_pe)
            mm_p = None
            for kc in range(DC):
                w0 = ([eg_cast["wp"], ev_o, ev_nz_last, egp.get(("s", par))]
                      if kc == 0 else [])
                mm_p = E("tensor", lambda eng, mc=mc, kc=kc, bank_p=bank_p,
                         ring=ring: eng.matmul(
                             bank_p[:, :],
                             wp_b[:, kc * 1024 + mc * 128: kc * 1024 + (mc + 1) * 128],
                             ring[:, kc * 512:(kc + 1) * 512],
                             start=(kc == 0), stop=(kc == DC - 1)),
                         waits=w0, inc=s_pe)
            ev_t1a = E("vector", lambda eng, mc=mc, bank_w=bank_w: eng.tensor_tensor(
                eg[0][:, 0:256], bank_w[:, 0:256], cpart[:, mc * 256:(mc + 1) * 256],
                ALU.add), waits=[mm_w, ev_cpart, egp.get("h")], inc=s_dv)
            ev_t1b = E("vector", lambda eng, mc=mc, bank_w=bank_w: eng.tensor_tensor(
                eg[0][:, 256:512], bank_w[:, 256:512],
                cpart[:, mc * 256:(mc + 1) * 256], ALU.add), inc=s_dv)
            egp[("t1", par)] = ev_t1b
            ev_h = E("scalar", lambda eng, mc=mc: eng.activation(
                eg[1][:, :], eg[0][:, :], AF.Tanh, scale=f32(1.0),
                bias=bias_t["bo"][:, mc:mc + 1]),
                waits=[ev_t1b, egp.get("g")], inc=s_ac)
            egp["h"] = ev_h
            ev_th = E("scalar", lambda eng: eng.activation(
                eg[2][:, :], eg[1][:, :], AF.Tanh, scale=f32(0.5)),
                waits=[egp.get("sgm")], inc=s_ac)
            ev_sg = E("vector", lambda eng: eng.tensor_scalar(
                eg[3][:, :], eg[2][:, :], 0.5, 0.5, ALU.mult, ALU.add),
                waits=[ev_th], inc=s_dv)
            ev_g = E("vector", lambda eng: eng.tensor_tensor(
                eg[2][:, :], eg[3][:, :], eg[1][:, :], ALU.mult), inc=s_dv)
            egp["g"] = ev_g
            ev_s = E("scalar", lambda eng, mc=mc, bank_p=bank_p: eng.activation(
                eg[4][:, :], bank_p[:, :], AF.Identity, scale=f32(1.0),
                bias=bias_t["bp"][:, mc:mc + 1]),
                waits=[mm_p, egp.get("sgm")], inc=s_ac)
            ev_sgm = E("vector", lambda eng: eng.tensor_tensor(
                eg[3][:, :], eg[4][:, :], eg[2][:, :], ALU.mult),
                waits=[ev_s, ev_g], inc=s_dv)
            egp[("s", par)] = ev_sgm
            egp["sgm"] = ev_sgm
            ev_fin = E("vector", lambda eng, xi=xi: eng.tensor_tensor(
                eg[6][:, :], eg[3][:, :], egx[xi][:, :], ALU.add),
                waits=[ev_x, ev_out_last], inc=s_dv)
            egp[("fin", xi)] = ev_fin
            ev_out_last = E("sync", lambda eng, qc=qc, mc=mc: eng.dma_start(
                out=out_d[mc * 128:(mc + 1) * 128, qc * 512:(qc + 1) * 512],
                in_=eg[6][:, :]), waits=[ev_fin], inc=s_out, inc_k=16)
        ost_ev[qc] = ev_out_last
        ring_rd[qc] = mm_p

    E("sync", lambda eng: eng.wait_ge(s_out.h, s_out.n), waits=[(s_out, s_out.n)])

    with nc.Block() as block:
        P.lower(block)
    nc._sem_cm.close()
    return nc


_NC_CACHE = {}


def kernel(L0_attn_x, L0_attn_w_qkv, L0_attn_w_proj, L0_attn_b_proj,
           L0_ffn_wf, L0_ffn_uf, L0_ffn_bf, L0_ffn_wi, L0_ffn_ui, L0_ffn_bi,
           L0_ffn_wc, L0_ffn_uc, L0_ffn_bc, L0_ffn_wo, L0_ffn_uo, L0_ffn_bo,
           L0_ffn_in_c):
    x = np.ascontiguousarray(np.asarray(L0_attn_x, dtype=np.float32))
    cT = lambda a: np.ascontiguousarray(np.asarray(a, dtype=np.float32).T)
    bt = lambda a: np.ascontiguousarray(
        np.asarray(a, dtype=np.float32).reshape(8, 128).T)
    shared = {
        "wqkvT": cT(L0_attn_w_qkv),
        "wfT": cT(L0_ffn_wf), "wiT": cT(L0_ffn_wi), "wcT": cT(L0_ffn_wc),
        "ufT": cT(L0_ffn_uf), "uiT": cT(L0_ffn_ui), "ucT": cT(L0_ffn_uc),
        "woT": cT(L0_ffn_wo), "uoT": cT(L0_ffn_uo), "wpT": cT(L0_attn_w_proj),
        "bf_t": bt(L0_ffn_bf), "bi_t": bt(L0_ffn_bi), "bc_t": bt(L0_ffn_bc),
        "bo_t": bt(L0_ffn_bo), "bp_t": bt(L0_attn_b_proj),
        "inc_t": bt(L0_ffn_in_c),
    }
    in_maps = []
    for j in range(NCORES):
        xTj = np.ascontiguousarray(
            x[:, j * I:(j + 1) * I, :].transpose(2, 0, 1)).reshape(D, NR)
        m = {"xT": xTj}
        m.update(shared)
        in_maps.append(m)

    if "nc" not in _NC_CACHE:
        _NC_CACHE["nc"] = build_program()
    nc = _NC_CACHE["nc"]
    res = run_bass_kernel_spmd(nc, in_maps, core_ids=list(range(NCORES)))
    outs = []
    for j in range(NCORES):
        o = np.asarray(res.results[j]["out"])
        outs.append(o.reshape(D, T, I).transpose(1, 2, 0))
    return np.ascontiguousarray(np.concatenate(outs, axis=1))
